# version 19 (final)
# baseline (speedup 1.0000x reference)
import numpy as np
import ml_dtypes
from contextlib import ExitStack

import concourse.bass as bass
import concourse.mybir as mybir
from concourse.bass_utils import run_bass_kernel_spmd

F32 = mybir.dt.float32
BF16 = mybir.dt.bfloat16
I32 = mybir.dt.int32
AF = mybir.ActivationFunctionType
ALU = mybir.AluOpType
AX = mybir.AxisListType
BF16NP = ml_dtypes.bfloat16

TAPS = ()

B, S, D = 4, 4096, 1024
T = S // 2
DFF = 4096
HE = 2048
NEXP = 8
RSQ = float(1.0 / np.sqrt(128.0))
MAGIC = float(0x5F3759DF)

ENGINES = ("tensor", "scalar", "vector", "gpsimd", "sync")
DMA_QUEUES = ("sync", "gpsimd", "scalar")
N_SLOTS = {"sync": 12, "gpsimd": 12, "scalar": 4}


class Prog:

    def __init__(self):
        self.ops = []
        self.writer = {}
        self.readers = {}
        self.dma_count = {q: 0 for q in DMA_QUEUES}
        self.epoch_key = None

    def add(self, engine, fn, reads=(), writes=(), dma=False, tag=""):
        deps = set()
        rds = list(reads)
        if self.epoch_key:
            rds.append(self.epoch_key)
        for r in rds:
            w = self.writer.get(r)
            if w is not None:
                deps.add(w)
        for wk in writes:
            w = self.writer.get(wk)
            if w is not None:
                deps.add(w)
            for rd in self.readers.get(wk, ()):
                deps.add(rd)
        i = len(self.ops)
        op = dict(engine=engine, fn=fn, deps=sorted(deps), dma=dma, tag=tag,
                  carrier=False, inc_count=None, slot=None, slot_use=None)
        if dma:
            assert engine in DMA_QUEUES
            k = self.dma_count[engine]
            self.dma_count[engine] += 1
            op["slot"] = (engine, k % N_SLOTS[engine])
            op["slot_use"] = k // N_SLOTS[engine] + 1
        self.ops.append(op)
        for wk in writes:
            self.writer[wk] = i
            self.readers[wk] = []
        for r in rds:
            self.readers.setdefault(r, []).append(i)
        return i

    def _fence_deps(self):
        last = {}
        for i, op in enumerate(self.ops):
            last[(op["engine"], op["slot"])] = i
        return sorted(set(last.values()))

    def epoch(self, name, fn):
        deps = self._fence_deps()
        i = len(self.ops)
        self.ops.append(dict(engine="vector", fn=fn, deps=deps, dma=False,
                             tag=f"epoch_{name}", carrier=False,
                             inc_count=None, slot=None, slot_use=None))
        key = f"__epoch_{name}"
        self.writer[key] = i
        self.readers[key] = []
        self.epoch_key = key

    def barrier_all(self, engine="sync"):
        self.ops.append(dict(engine=engine, fn=lambda eng: eng.nop(),
                             deps=self._fence_deps(), dma=False, tag="final",
                             carrier=False, inc_count=None, slot=None,
                             slot_use=None))

    def _schedule(self):
        for op in self.ops:
            for d in op["deps"]:
                self.ops[d]["carrier"] = True
        cum = {e: 0 for e in ENGINES}
        for op in self.ops:
            if not op["dma"] and op["carrier"]:
                cum[op["engine"]] += 1
                op["inc_count"] = cum[op["engine"]]
        dims = list(ENGINES) + [(q, s) for q in DMA_QUEUES
                                for s in range(N_SLOTS[q])]
        know = {e: {d: 0 for d in dims} for e in ENGINES}
        op_know = [None] * len(self.ops)

        def dep_id(dop):
            if dop["dma"]:
                return dop["slot"], dop["slot_use"]
            return dop["engine"], dop["inc_count"]

        for i, op in enumerate(self.ops):
            e = op["engine"]
            ke = know[e]
            need = {}
            for d in op["deps"]:
                dim, c = dep_id(self.ops[d])
                need[dim] = max(need.get(dim, 0), c)
            if op["dma"] and op["slot_use"] > 1:
                s = op["slot"]
                need[s] = max(need.get(s, 0), op["slot_use"] - 1)
            waits = {dim: c for dim, c in need.items() if ke[dim] < c}
            op["waits"] = waits
            for dim, c in waits.items():
                ke[dim] = max(ke[dim], c)
            for d in op["deps"]:
                dk = op_know[d]
                for dim in dims:
                    if dk[dim] > ke[dim]:
                        ke[dim] = dk[dim]
            my = dict(ke)
            if op["dma"]:
                my[op["slot"]] = max(my[op["slot"]], op["slot_use"])
            elif op["carrier"]:
                my[e] = max(my[e], op["inc_count"])
                if e == "tensor":
                    ke[e] = my[e]
            op_know[i] = my

    def emit(self, nc, st):
        self._schedule()
        sems = {}
        for e in ENGINES:
            sems[e] = st.enter_context(nc.semaphore(f"sem_{e}"))
        for q in DMA_QUEUES:
            for s in range(N_SLOTS[q]):
                sems[(q, s)] = st.enter_context(nc.semaphore(f"ds_{q}_{s}"))
        per_engine = {e: [] for e in ENGINES}
        for op in self.ops:
            per_engine[op["engine"]].append(op)
        block = st.enter_context(nc.Block())

        def make_body(ops):
            def body(eng):
                for op in ops:
                    for dim, c in sorted(op["waits"].items(), key=str):
                        mult = 16 if isinstance(dim, tuple) else 1
                        eng.wait_ge(sems[dim], c * mult)
                    inst = op["fn"](eng)
                    if op["dma"]:
                        inst.then_inc(sems[op["slot"]], 16)
                    elif op["carrier"]:
                        inst.then_inc(sems[op["engine"]], 1)
            return body

        for e in ENGINES:
            if per_engine[e]:
                getattr(block, e)(make_body(per_engine[e]))


def build_nc():
    nc = bass.Bass("TRN2", target_bir_lowering=False)
    p = Prog()
    A = p.add
    dram = {}

    def din(name, shape, dt=F32):
        dram[name] = nc.dram_tensor(name, list(shape), dt, kind="ExternalInput")

    din("x_full", (S, D)); din("x_own", (T, D))
    din("sel0", (128, 1)); din("sel1", (128, 1))
    din("ident", (128, 128), BF16); din("ones_row", (1, 128), BF16)
    for nm in ("lng_c", "lnb_c", "ln1g_c", "ln1b_c", "ln2g_c", "ln2b_c",
               "bq_c", "bk_c"):
        din(nm, (128, 8))
    din("b1_c", (128, 32))
    din("rb1_c", (128, 128)); din("rb2_c", (128, 128))
    din("sb1_c", (128, 32)); din("sb2_c", (128, 32))
    din("lnfg_rep", (128, 1024)); din("lnfb_rep", (128, 1024))
    din("bgate_row", (1, 1024)); din("bv_row", (1, 1024)); din("bo_row", (1, 1024))
    din("b2_row", (1, 1024)); din("bf_row", (1, 1024))
    din("rb3", (NEXP, 1024)); din("sb3", (2, 1024)); din("gateb_row", (1, 8))
    din("A_f", (16, 16)); din("bdelta_c", (16, 1)); din("bin_c", (16, 1))
    din("wdelta_r", (128, 128)); din("win_r", (128, 128)); din("gatew_r", (128, 64))
    din("C_w", (16, 1024))
    din("Wq", (D, D)); din("Wk", (D, D)); din("Wv", (D, D)); din("Wo", (D, D))
    din("Wgate", (D, D)); din("W1", (D, DFF)); din("W2", (DFF, D))
    din("Wf", (2 * D, D))
    din("rw1", (NEXP, D, HE)); din("rw2", (NEXP, D, HE)); din("rw3", (NEXP, HE, D))
    din("sw1", (2, D, HE)); din("sw2", (2, D, HE)); din("sw3", (2, HE, D))

    out_h = nc.dram_tensor("out", [T, D], F32, kind="ExternalOutput")
    ytT_d = nc.dram_tensor("ytT_d", [D, S], BF16, kind="Internal")
    ctxT_d = nc.dram_tensor("ctxT_d", [D, T], BF16, kind="Internal")
    x2_d = nc.dram_tensor("x2_d", [T, D], F32, kind="Internal")
    out1_d = nc.dram_tensor("out1_d", [T, D], F32, kind="Internal")
    moT_d = nc.dram_tensor("moT_d", [2 * D, T], BF16, kind="Internal")
    y2T_d = nc.dram_tensor("y2T_d", [D, T], BF16, kind="Internal")
    gate_d = nc.dram_tensor("gate_d", [T, D], BF16, kind="Internal")

    taps = {}
    tap_specs = dict(
        ytT=([D, S], BF16), dltT=([16, S], BF16), btT=([16, S], F32),
        hall=([16, S], BF16), ctxT=([D, T], BF16), x2=([T, D], F32),
        moT=([2 * D, T], BF16), out1=([T, D], F32), negm=([128, 8], F32),
        wts=([128, 128], F32), ffn=([128, 8 * 1024], F32),
        y2T=([D, T], BF16))
    for nm in TAPS:
        shp, dt = tap_specs[nm]
        taps[nm] = nc.dram_tensor("tap_" + nm, shp, dt, kind="ExternalOutput")

    st0 = ExitStack()
    with st0:
        def sb(name, shape, dt, side=None):
            return st0.enter_context(
                nc.sbuf_tensor(name, list(shape), dt, side=side))

        psA = st0.enter_context(nc.psum_tensor("psA", [128, 1024], F32))
        psB = st0.enter_context(nc.psum_tensor("psB", [128, 1024], F32))
        ps5a = st0.enter_context(nc.psum_tensor("ps5a", [128, 512], F32))
        ps5b = st0.enter_context(nc.psum_tensor("ps5b", [128, 512], F32))
        psT = st0.enter_context(nc.psum_tensor("psT", [128, 512], F32))
        psS = st0.enter_context(nc.psum_tensor("psS", [16, 1], F32))
        PA0, PA1, PB0, PB1 = "pbA0", "pbA1", "pbB0", "pbB1"
        P5A, P5B = "pb5a", "pb5b"
        PT, PSC = "pbT", "pbSc"

        ident = sb("identc", (128, 128), BF16)
        ones_row = sb("onesr", (1, 128), BF16)
        ones_col = sb("onesc", (128, 1), BF16)
        sel0 = sb("sel0c", (128, 1), F32)
        sel1 = sb("sel1c", (128, 1), F32)
        lnp = {nm: sb("c" + nm, (128, 8), F32)
               for nm in ("lng", "lnb", "ln1g", "ln1b", "ln2g", "ln2b",
                          "bq", "bk")}
        bqs = sb("bqs", (128, 8), F32)
        b1c = sb("b1cs", (128, 32), F32)
        rb1c = sb("rb1cs", (128, 128), F32)
        rb2c = sb("rb2cs", (128, 128), F32)
        sb1c = sb("sb1cs", (128, 32), F32)
        sb2c = sb("sb2cs", (128, 32), F32)
        NROW = 1024 * 5 + 8
        rows_b = sb("rows_b", (1, NROW), BF16)
        ROFF = dict(bgate=0, bv=1024, bo=2048, b2=3072, bf=4096,
                    gateb=5120)
        A_f = sb("A_fs", (16, 16), F32)
        bdelta = sb("bdel", (16, 1), F32)
        hbdel = sb("hbdel", (16, 1), F32)
        binc = sb("bincs", (16, 1), F32)
        wdelta = sb("wdels", (128, 128), BF16)
        win = sb("wins", (128, 128), BF16)
        gatew = sb("gatews", (128, 64), BF16)
        C_bf = sb("C_bfs", (16, 1024), BF16)
        negm = sb("negms", (128, 8), F32)
        wts = sb("wtss", (128, 16 * 8), F32)
        smr = sb("smrs", (1, 64), F32)
        smi = sb("smis", (1, 8), F32)
        smb = sb("smbs", (1, 1), BF16)
        epb = sb("epbs", (1, 4), F32)
        xb = [sb(f"xb{i}", (128, 1024), F32) for i in range(2)]
        yb = [sb(f"yb{i}", (128, 1024), F32) for i in range(2)]
        dump = [sb(f"dmpb{i}", (128, 1024), BF16) for i in range(2)]
        ybf = [sb(f"ybf{i}", (128, 1024), BF16) for i in range(2)]
        st1 = [sb(f"sta{i}", (128, 1), F32) for i in range(4)]
        sw1 = sb("sw1s", (128, 8), F32)
        sw2 = sb("sw2s", (128, 8), F32)
        sw3 = sb("sw3s", (128, 8), F32)
        swi = sb("swis", (128, 8), I32)
        dump2 = sb("dmp2b", (128, 1024), BF16)
        st2 = [sb(f"stc{i}", (128, 1), F32) for i in range(4)]
        st3 = [sb(f"std{i}", (128, 1), F32) for i in range(4)]
        sti = [sb(f"sti{i}", (128, 1), I32) for i in range(4)]
        tstage = sb("tstage", (128, 1024), BF16)

        stRs = ExitStack()
        sbrs = lambda n, sh, dt: stRs.enter_context(
            nc.sbuf_tensor(n, list(sh), dt, side="right"))
        dltT = sbrs("dltTs", (16, S), BF16)
        btT = sbrs("btTs", (16, S), F32)
        hall = sbrs("halls", (16, S), BF16)
        hown = sbrs("howns", (16, T), BF16)
        asc = [sbrs(f"ascs{i}", (16, 256 * 16), BF16) for i in range(2)]
        dtmp = sbrs("dtmpS", (16, 512), F32)
        hz = sbrs("hzS", (16, 1), BF16)
        stR = ExitStack()
        sbr = lambda n, sh, dt: stR.enter_context(
            nc.sbuf_tensor(n, list(sh), dt, side="right"))
        ytg = [sbr(f"ytg{i}", (128, 4096), BF16) for i in range(2)]
        ytg2 = sbr("ytg2b", (128, 4096), BF16)
        xnb = [sbr(f"xnbb{i}", (128, 4096), BF16) for i in range(2)]
        yos = sbr("yosb", (128, 4096), BF16)

        def dma(queue, dst_ap, src_ap, reads, writes, tag=""):
            A(queue, lambda e, d=dst_ap, s=src_ap: e.dma_start(out=d, in_=s),
              reads=list(reads), writes=list(writes), dma=True, tag=tag)

        trb_state = [0]

        def transpose8(srcbuf, srckey, dst_round_view, dstkey,
                       extra_reads=(), banks=None, evac="vector"):
            if banks is None:
                banks = [(psT, PT, 0)]
            dks = dstkey if isinstance(dstkey, list) else [dstkey, dstkey]
            for r in range(2):
                ps_, bkk, off = banks[trb_state[0] % len(banks)]
                trb_state[0] += 1
                for q in range(4):
                    c = r * 4 + q
                    A("tensor",
                      lambda e, c=c, q=q, srcbuf=srcbuf, ps_=ps_, off=off:
                      e.matmul(
                          ps_[:, off + q * 128:off + (q + 1) * 128],
                          srcbuf[:, c * 128:(c + 1) * 128], ident[:],
                          start=True, stop=True),
                      reads=[srckey, "ident", bkk], writes=[bkk], tag="tr8")
                if evac == "vector":
                    A("vector", lambda e, r=r, ps_=ps_, off=off:
                      e.tensor_copy(
                        dst_round_view(r),
                        ps_[:, off:off + 512].rearrange("p (c t) -> p c t",
                                                        c=4)),
                      reads=[bkk] + list(extra_reads),
                      writes=[dks[r], bkk], tag="tr8ev")
                else:
                    A("scalar", lambda e, r=r, ps_=ps_, off=off:
                      e.activation(
                        dst_round_view(r),
                        ps_[:, off:off + 512].rearrange("p (c t) -> p c t",
                                                        c=4), AF.Identity),
                      reads=[bkk] + list(extra_reads),
                      writes=[dks[r], bkk], tag="tr8ev")

        TRB5 = [(psT, PT, 0), (psA, PA0, 0), (psA, PA1, 512),
                (psB, PB0, 0), (psB, PB1, 512)]
        TRB3 = [(psT, PT, 0), (ps5a, P5A, 0), (ps5b, P5B, 0)]

        def tap_copy(name, src_ap, rkeys):
            if name in taps:
                dma("sync", taps[name].ap(), src_ap, rkeys, [f"tap_{name}"],
                    tag=f"tap{name}")

        def rsqrt_small(vap, vkey, yap, iap, scratch_ap, wkeys):
            A("vector", lambda e: e.tensor_scalar(
                iap, vap.bitcast(I32), 1, None, ALU.logical_shift_right),
              reads=[vkey] + wkeys, writes=wkeys)
            A("vector", lambda e: e.tensor_scalar(
                iap, iap, -1, int(MAGIC), ALU.mult, ALU.add),
              reads=wkeys, writes=wkeys)
            A("vector", lambda e: e.tensor_copy(yap, iap.bitcast(F32)),
              reads=wkeys, writes=wkeys)
            for _ in range(2):
                A("vector", lambda e: e.tensor_mul(scratch_ap, yap, yap),
                  reads=wkeys, writes=wkeys)
                A("vector", lambda e: e.tensor_mul(scratch_ap, scratch_ap,
                                                   vap),
                  reads=wkeys + [vkey], writes=wkeys)
                A("vector", lambda e: e.tensor_scalar(
                    scratch_ap, scratch_ap, -0.5, 1.5, ALU.mult, ALU.add),
                  reads=wkeys, writes=wkeys)
                A("vector", lambda e: e.tensor_mul(yap, yap, scratch_ap),
                  reads=wkeys, writes=wkeys)

        def ln_tile(xkey, xap, slot, dmpsel, n=1024):
            s_sum, s_ss, s_rs, s_i = st1[slot], st2[slot], st3[slot], sti[slot]
            dmp = dump[dmpsel]
            kw = f"LNS{slot}"
            dk = f"dmp{dmpsel}"
            A("vector", lambda e: e.scalar_tensor_tensor(
                dmp[:, 0:n], xap, 1.0, xap, ALU.mult, ALU.mult,
                accum_out=s_ss[:, 0:1]),
              reads=[xkey, kw], writes=[kw, dk], tag="ln_sq")
            A("vector", lambda e: e.tensor_scalar(
                dmp[:, 0:n], xap, 1.0, 0.0, ALU.mult, ALU.add,
                accum_out=s_sum[:, 0:1]),
              reads=[xkey, kw], writes=[kw, dk], tag="ln_sum")
            A("vector", lambda e: e.tensor_scalar_mul(
                s_sum[:, 0:1], s_sum[:, 0:1], 1.0 / n),
              reads=[kw], writes=[kw])
            A("vector", lambda e: e.tensor_mul(s_rs[:, 0:1], s_sum[:, 0:1],
                                               s_sum[:, 0:1]),
              reads=[kw], writes=[kw])
            A("vector", lambda e: e.scalar_tensor_tensor(
                s_ss[:, 0:1], s_ss[:, 0:1], 1.0 / n, s_rs[:, 0:1],
                ALU.mult, ALU.subtract),
              reads=[kw], writes=[kw])
            A("vector", lambda e: e.tensor_scalar_add(s_ss[:, 0:1],
                                                      s_ss[:, 0:1], 1e-5),
              reads=[kw], writes=[kw])
            rsqrt_small(s_ss[:, 0:1], kw, s_rs[:, 0:1], s_i[:, 0:1],
                        s_i[:, 0:1].bitcast(F32), [kw])
            A("vector", lambda e: e.scalar_tensor_tensor(
                s_sum[:, 0:1], s_rs[:, 0:1], -1.0, s_sum[:, 0:1],
                ALU.mult, ALU.mult),
              reads=[kw], writes=[kw], tag="nmr")
            return s_sum, s_rs, kw

        def ln_group(xkeys, xaps, dmpsel, n=1024):
            k = len(xaps)
            co = dmpsel * 4
            K1, K2, K3 = f"LW1{dmpsel}", f"LW2{dmpsel}", f"LW3{dmpsel}"
            dk = f"dmp{dmpsel}"
            dmp = dump[dmpsel]
            for j, (xk, xap) in enumerate(zip(xkeys, xaps)):
                A("scalar", lambda e, xap=xap, j=j: e.activation(
                    dmp[:, 0:n], xap, AF.Square,
                    accum_out=sw2[:, co + j:co + j + 1]),
                  reads=[xk], writes=[K2, dk], tag="lng_sq")
                A("vector", lambda e, xap=xap, j=j: e.tensor_scalar(
                    dump2[:, 0:n], xap, 1.0, 0.0, ALU.mult, ALU.add,
                    accum_out=sw1[:, co + j:co + j + 1]),
                  reads=[xk], writes=[K1, "dmp2"], tag="lng_sum")
            A("vector", lambda e: e.tensor_scalar_mul(sw1[:, co:co + k],
                                                      sw1[:, co:co + k],
                                                      1.0 / n),
              reads=[K1], writes=[K1])
            A("vector", lambda e: e.tensor_mul(sw3[:, co:co + k],
                                               sw1[:, co:co + k],
                                               sw1[:, co:co + k]),
              reads=[K1], writes=[K3])
            A("vector", lambda e: e.scalar_tensor_tensor(
                sw2[:, co:co + k], sw2[:, co:co + k], 1.0 / n,
                sw3[:, co:co + k], ALU.mult, ALU.subtract),
              reads=[K2, K3], writes=[K2])
            A("vector", lambda e: e.tensor_scalar_add(sw2[:, co:co + k],
                                                      sw2[:, co:co + k],
                                                      1e-5),
              reads=[K2], writes=[K2])
            rsqrt_small(sw2[:, co:co + k], K2, sw3[:, co:co + k],
                        swi[:, co:co + k], swi[:, co:co + k].bitcast(F32),
                        [K3])
            A("vector", lambda e: e.scalar_tensor_tensor(
                sw1[:, co:co + k], sw3[:, co:co + k], -1.0,
                sw1[:, co:co + k], ALU.mult, ALU.mult),
              reads=[K1, K3], writes=[K1], tag="nmrw")
            return sw1, sw3, [K1, K3], co

        def row_bf(key, off, ln_):
            return rows_b[:, ROFF[key] + off: ROFF[key] + off + ln_]

        scan_state = dict(t=0, built=0, dgroups=0, tctr=0, inpump=False,
                          credit=0.0, target=600.0)
        TOP_COST = dict(smm=216, cmm=85, kmm=216, vmm=115, qmm=216, dmm=220,
                        bmm=220, womm=216, wob=216, gmm=216, gb=216, hC=216,
                        tr8=85, x2tr=85, ctr=85, w1mm=216, w2mm=216, b2b=216,
                        wfmm=216, bfb=216, p1mm=216, p2mm=216, w3mm=216,
                        b3b=216, mbc=60, nmm=216, gsc=60, gscb=60, vb=115)

        _A_raw = p.add

        VS_COST = dict(lng_sq=900, lng_sum=900, yhat=900, x2h=900, fnorm=900,
                       y2h=900, fg=900, fbias=900, o1=900, gsel0=1500,
                       gsel1=1500, p4y=900, bmy=900, fev=500, tr8ev=500,
                       xn2ev=500, exp=800, kev=400, vev=300, qev=400,
                       xnT=500, xn1T=500, qxn=500, gxn=500, relu=400,
                       momul=500, moadd=900, mocast=500, ln_sq=900,
                       ln_sum=900)

        def A(engine, fn, reads=(), writes=(), dma=False, tag=""):
            idx = _A_raw(engine, fn, reads=reads, writes=writes, dma=dma,
                         tag=tag)
            if scan_state["inpump"] or tag in ("scmm", "sctanh", "ascb"):
                return idx
            if engine == "tensor":
                scan_state["credit"] += TOP_COST.get(tag, 150)
            elif (engine in ("vector", "scalar")
                  and scan_state["target"] <= 600.0):
                scan_state["credit"] += VS_COST.get(tag, 260)
            else:
                return idx
            if scan_state["credit"] >= scan_state["target"]:
                scan_state["inpump"] = True
                n = min(int(scan_state["credit"] // scan_state["target"]), 2)
                scan_state["credit"] -= n * scan_state["target"]
                pump_scan(n)
                scan_state["inpump"] = False
            return idx

        def build_asc_chunk(k):
            ring = asc[k % 2]
            rk = f"ascr{k % 2}"
            gsl = slice(k * 256, (k + 1) * 256)
            for j in range(16):
                A("vector", lambda e, j=j, ring=ring, gsl=gsl:
                  e.tensor_scalar_mul(
                      ring[:].rearrange("p (t j) -> p t j", j=16)[:, :, j],
                      dltT[:, gsl], A_f[:, j:j + 1]),
                  reads=[f"dltT{k // 2}", "A_f"], writes=[rk], tag="ascb")

        def pump_scan(n):
            for _ in range(n):
                t = scan_state["t"]
                if t >= S:
                    return
                k = t // 256
                if k >= scan_state["built"]:
                    if k // 2 >= scan_state["dgroups"]:
                        return
                    build_asc_chunk(k)
                    scan_state["built"] = k + 1
                ring = asc[k % 2]
                rk = f"ascr{k % 2}"
                hprev = hz[:, 0:1] if t == 0 else hall[:, t - 1:t]
                hprevk = "hz" if t == 0 else f"h{t - 1}"
                off = (t % 256) * 16
                A("tensor", lambda e, ring=ring, off=off, hp=hprev: e.matmul(
                    psS[0:16, 0:1], ring[:, off:off + 16], hp,
                    start=True, stop=True),
                  reads=[rk, hprevk, PSC], writes=[PSC], tag="scmm")
                A("scalar", lambda e, t=t: e.activation(
                    hall[:, t:t + 1], psS[0:16, 0:1], AF.Tanh,
                    bias=btT[:, t:t + 1]),
                  reads=[PSC, f"btT{t // 512}"], writes=[f"h{t}", PSC],
                  tag="sctanh")
                scan_state["t"] = t + 1

        dma("sync", ident[:], dram["ident"].ap(), [], ["ident"])
        dma("sync", ones_row[:], dram["ones_row"].ap(), [], ["ones_row"])
        dma("sync", sel0[:], dram["sel0"].ap(), [], ["sel0"])
        dma("sync", sel1[:], dram["sel1"].ap(), [], ["sel1"])
        for nm, t_ in (("lng", "lng_c"), ("lnb", "lnb_c"), ("ln1g", "ln1g_c"),
                       ("ln1b", "ln1b_c"), ("ln2g", "ln2g_c"),
                       ("ln2b", "ln2b_c"), ("bq", "bq_c"), ("bk", "bk_c")):
            dma("sync", lnp[nm][:], dram[t_].ap(), [], [nm])
        dma("sync", b1c[:], dram["b1_c"].ap(), [], ["b1c"])
        dma("sync", rb1c[:], dram["rb1_c"].ap(), [], ["rb1c"])
        dma("sync", rb2c[:], dram["rb2_c"].ap(), [], ["rb2c"])
        dma("sync", sb1c[:], dram["sb1_c"].ap(), [], ["sb1c"])
        dma("sync", sb2c[:], dram["sb2_c"].ap(), [], ["sb2c"])
        for nm, key in (("bgate_row", "bgate"), ("bv_row", "bv"),
                        ("bo_row", "bo"), ("b2_row", "b2"), ("bf_row", "bf"),
                        ("gateb_row", "gateb")):
            ln_ = dram[nm].shape[1]
            dma("gpsimd", rows_b[:, ROFF[key]:ROFF[key] + ln_],
                dram[nm].ap(), ["rows_b"], ["rows_b"])
        dma("sync", A_f[:], dram["A_f"].ap(), [], ["A_f"])
        dma("sync", bdelta[:], dram["bdelta_c"].ap(), [], ["bdeltak"])
        dma("sync", binc[:], dram["bin_c"].ap(), [], ["binck"])
        dma("gpsimd", wdelta[:], dram["wdelta_r"].ap(), [], ["wdelta"])
        dma("gpsimd", win[:], dram["win_r"].ap(), [], ["win"])
        dma("gpsimd", gatew[:], dram["gatew_r"].ap(), [], ["gatew"])
        dma("gpsimd", C_bf[:], dram["C_w"].ap(), [], ["C_bf"])
        A("vector", lambda e: e.memset(ones_col[:], 1.0), writes=["ones_col"])
        A("vector", lambda e: e.memset(hz[:], 0.0), writes=["hz"])
        A("vector", lambda e: e.tensor_scalar_mul(bqs[:], lnp["bq"][:], RSQ),
          reads=["bq"], writes=["bqs"])
        A("vector", lambda e: e.tensor_scalar_mul(hbdel[:], bdelta[:], 0.5),
          reads=["bdeltak"], writes=["hbdel"])

        st23 = ExitStack()
        sbq = lambda n, sh, dt: st23.enter_context(
            nc.sbuf_tensor(n, list(sh), dt))
        kT = sbq("kTs", (128, 2 * S), BF16)
        vsb = sbq("vsbs", (128, 32 * 258), BF16)
        wkh = sbq("wkhs", (128, 8 * 256), BF16)
        wvh = sbq("wvhs", (128, 8 * 256), BF16)
        stP1 = ExitStack()
        xbig = stP1.enter_context(
            nc.sbuf_tensor("xbigs", [128, 4096], F32, side="right"))

        def emit_kv_group(hg, g, gb):
            xnk = f"xnb{gb}"
            for c in range(8):
                A("scalar", lambda e, b_=gb, c=c: e.activation(
                    xnb[b_][:, c * 512:(c + 1) * 512],
                    ytg[b_][:, c * 512:(c + 1) * 512], AF.Identity,
                    bias=lnp["ln1b"][:, c:c + 1],
                    scale=lnp["ln1g"][:, c:c + 1]),
                  reads=[f"ytg{gb}_h{c // 4}", "ln1g", "ln1b",
                         xnk + f"_{c}"],
                  writes=[xnk + f"_{c}"], tag="xn1T")
            for fc in range(2):
                pse, pk = (ps5a, P5A) if fc == 0 else (ps5b, P5B)
                for c in range(8):
                    A("tensor", lambda e, b_=gb, c=c, fc=fc, pse=pse:
                      e.matmul(pse[:, 0:512],
                               wkh[:, c * 256 + fc * 128:
                                   c * 256 + (fc + 1) * 128],
                               xnb[b_][:, c * 512:(c + 1) * 512],
                               start=(c == 0), stop=(c == 7)),
                      reads=[xnk + f"_{c}", "wkh", pk], writes=[pk],
                      tag="kmm")
                A("vector", lambda e, fc=fc, g=g, pse=pse, hg=hg:
                  e.tensor_scalar(kT[:, fc * S + g * 512:
                                     fc * S + (g + 1) * 512],
                                  pse[:, 0:512],
                                  lnp["bk"][:, hg * 2 + fc:
                                            hg * 2 + fc + 1], None,
                                  ALU.add),
                  reads=[pk, "bk", f"kT{fc}_{g}"],
                  writes=[f"kT{fc}_{g}", pk], tag="kev")
            for j2 in range(4):
                i2 = g * 4 + j2
                pse, pk = (ps5a, P5A) if j2 % 2 == 0 else (ps5b, P5B)
                for c in range(8):
                    A("tensor", lambda e, b_=gb, c=c, j2=j2, pse=pse:
                      e.matmul(pse[:, 0:256],
                               xnb[b_][:, c * 512 + j2 * 128:
                                   c * 512 + (j2 + 1) * 128],
                               wvh[:, c * 256:(c + 1) * 256],
                               start=(c == 0), stop=False),
                      reads=[xnk + f"_{c}", "wvh", pk], writes=[pk],
                      tag="vmm")
                A("tensor", lambda e, pse=pse, hg=hg: e.matmul(
                    pse[:, 0:256], ones_row[:],
                    row_bf("bv", hg * 256, 256), start=False, stop=True),
                  reads=["ones_row", "rows_b", pk], writes=[pk], tag="vb")
                A("vector", lambda e, i2=i2, pse=pse: e.tensor_copy(
                    vsb[:, i2 * 258:(i2 + 1) * 258].rearrange(
                        "p (h c) -> p h c", h=2)[:, :, 0:128],
                    pse[:, 0:256].rearrange("p (h c) -> p h c", h=2)),
                  reads=[pk, f"vsb{i2}"], writes=[f"vsb{i2}", pk],
                  tag="vev")

        dma("gpsimd", wkh[:].rearrange("p (c f) -> p c f", c=8),
            dram["Wk"].ap()[:, 0:256].rearrange("(c p) f -> p c f", p=128),
            ["wkh"], ["wkh"], tag="wk")
        dma("gpsimd", wvh[:].rearrange("p (c f) -> p c f", c=8),
            dram["Wv"].ap()[:, 0:256].rearrange("(c p) f -> p c f", p=128),
            ["wvh"], ["wvh"], tag="wv")
        A("vector", lambda e: e.memset(
            vsb[:].rearrange("p (i h c) -> p i h c", i=32, h=2)[
                :, :, :, 128:129], 1.0),
          reads=[f"vsb{i}" for i in range(32)],
          writes=[f"vsb{i}" for i in range(32)], tag="vones")
        lng_ret = {}

        def p1_load_ln(g):
            for j in range(4):
                i = g * 4 + j
                dma("sync", xbig[:, j * 1024:(j + 1) * 1024],
                    dram["x_full"].ap()[i * 128:(i + 1) * 128, :],
                    [f"xbg{j}"], [f"xbg{j}"], tag=f"p1x{i}")
            lng_ret[g] = ln_group(
                [f"xbg{j}" for j in range(4)],
                [xbig[:, j * 1024:(j + 1) * 1024]
                 for j in range(4)], g % 2)

        p1_load_ln(0)
        for g in range(8):
            gb = g % 2
            nmrw, rstdw, lks, lco = lng_ret[g]
            for j in range(4):
                i = g * 4 + j
                bsel = i % 2
                ybk = f"ybf{bsel}"
                A("vector", lambda e, b_=bsel, j=j, lco=lco:
                  e.tensor_scalar(
                    ybf[b_][:], xbig[:, j * 1024:(j + 1) * 1024],
                    rstdw[:, lco + j:lco + j + 1],
                    nmrw[:, lco + j:lco + j + 1],
                    ALU.mult, ALU.add),
                  reads=[f"xbg{j}"] + lks, writes=[ybk], tag="yhat")
                transpose8(ybf[bsel][:], ybk,
                           lambda r, b_=gb, j=j: ytg[b_][:].rearrange(
                               "p (c t) -> p c t", c=8)[
                               :, r * 4:(r + 1) * 4, j * 128:(j + 1) * 128],
                           [f"ytg{gb}_h0", f"ytg{gb}_h1"], banks=TRB5)
            dma("sync",
                ytT_d.ap().rearrange("(c p) s -> p c s", p=128)[
                    :, :, g * 512:(g + 1) * 512],
                ytg[gb][:].rearrange("p (c t) -> p c t", c=8),
                [f"ytg{gb}_h0", f"ytg{gb}_h1"], [f"ytTd{g}"], tag="yspill")
            xnk = f"xnb{gb}"
            for c in range(8):
                A("scalar", lambda e, b_=gb, c=c: e.activation(
                    xnb[b_][:, c * 512:(c + 1) * 512],
                    ytg[b_][:, c * 512:(c + 1) * 512], AF.Identity,
                    bias=lnp["lnb"][:, c:c + 1], scale=lnp["lng"][:, c:c + 1]),
                  reads=[f"ytg{gb}_h{c // 4}", "lng", "lnb", xnk + f"_{c}"],
                  writes=[xnk + f"_{c}"], tag="xnT")
            if g < 7:
                p1_load_ln(g + 1)
            for c in range(8):
                A("tensor", lambda e, b_=gb, c=c: e.matmul(
                    ps5a[0:16, 0:512], wdelta[:, c * 16:(c + 1) * 16],
                    xnb[b_][:, c * 512:(c + 1) * 512],
                    start=(c == 0), stop=(c == 7)),
                  reads=[xnk + f"_{c}", "wdelta", P5A], writes=[P5A],
                  tag="dmm")
            for c in range(8):
                A("tensor", lambda e, b_=gb, c=c: e.matmul(
                    ps5b[0:16, 0:512], win[:, c * 16:(c + 1) * 16],
                    xnb[b_][:, c * 512:(c + 1) * 512],
                    start=(c == 0), stop=(c == 7)),
                  reads=[xnk + f"_{c}", "win", P5B], writes=[P5B], tag="bmm")
            A("scalar", lambda e: e.activation(
                dtmp[:, 0:512], ps5a[0:16, 0:512], AF.Tanh,
                bias=hbdel[:, 0:1], scale=0.5),
              reads=[P5A, "hbdel", "dtmpk"], writes=["dtmpk", P5A],
              tag="dsig")
            A("vector", lambda e, g=g: e.tensor_scalar(
                dltT[:, g * 512:(g + 1) * 512], dtmp[:, 0:512], 0.5, 0.5,
                ALU.mult, ALU.add),
              reads=["dtmpk"], writes=[f"dltT{g}"], tag="dfin")
            A("vector", lambda e, g=g: e.tensor_scalar(
                btT[:, g * 512:(g + 1) * 512], ps5b[0:16, 0:512],
                binc[:, 0:1], None, ALU.add),
              reads=[P5B, "binck"], writes=[f"btT{g}", P5B], tag="btf")
            scan_state["dgroups"] = g + 1
            pump_scan(8)
            emit_kv_group(0, g, gb)

        tap_copy("ytT", ytT_d.ap(), [f"ytTd{g}" for g in range(8)])
        if "dltT" in taps:
            dma("sync", taps["dltT"].ap(), dltT[:],
                [f"dltT{g}" for g in range(8)], ["tap_dltT"])
        if "btT" in taps:
            dma("sync", taps["btT"].ap(), btT[:],
                [f"btT{g}" for g in range(8)], ["tap_btT"])

        scan_state["target"] = 600.0
        stP1.close()
        p.epoch("p23", lambda e: e.memset(epb[0:1, 0:1], 0.0))
        st23b = ExitStack()
        sbq2 = lambda n, sh, dt: st23b.enter_context(
            nc.sbuf_tensor(n, list(sh), dt))
        qT = sbq2("qTs", (128, 2 * T), BF16)
        expb = [sbq2(f"expbs{i}", (128, 1024), BF16) for i in range(3)]
        ctxe = sbq2("ctxes", (128, 512), BF16)
        ctxb = sbq2("ctxbs", (128, 512), BF16)

        for hg in range(4):
            if hg > 0:
                dma("gpsimd", wkh[:].rearrange("p (c f) -> p c f", c=8),
                    dram["Wk"].ap()[:, hg * 256:(hg + 1) * 256].rearrange(
                        "(c p) f -> p c f", p=128),
                    ["wkh"], ["wkh"], tag="wk")
                dma("gpsimd", wvh[:].rearrange("p (c f) -> p c f", c=8),
                    dram["Wv"].ap()[:, hg * 256:(hg + 1) * 256].rearrange(
                        "(c p) f -> p c f", p=128),
                    ["wvh"], ["wvh"], tag="wv")
                A("vector", lambda e: e.memset(
                    vsb[:].rearrange("p (i h c) -> p i h c", i=32, h=2)[
                        :, :, :, 128:129], 1.0),
                  reads=[f"vsb{i}" for i in range(32)],
                  writes=[f"vsb{i}" for i in range(32)], tag="vones")
                scan_state["target"] = 600.0
                for g in range(8):
                    gb = g % 2
                    for hf in range(2):
                        dma("sync", ytg[gb][:, hf * 2048:(hf + 1) * 2048],
                            ytT_d.ap().rearrange("(c p) s -> p c s", p=128)[
                                :, hf * 4:(hf + 1) * 4,
                                g * 512:(g + 1) * 512],
                            [f"ytg{gb}_h{hf}"], [f"ytg{gb}_h{hf}"],
                            tag="yld")
                    emit_kv_group(hg, g, gb)

            dma("gpsimd", wkh[:].rearrange("p (c f) -> p c f", c=8),
                dram["Wq"].ap()[:, hg * 256:(hg + 1) * 256].rearrange(
                    "(c p) f -> p c f", p=128),
                ["wkh"], ["wkh"], tag="wq")
            for go in range(4):
                dma("sync", ytg[0][:],
                    ytT_d.ap().rearrange("(c p) s -> p c s", p=128)[
                        :, :, go * 512:(go + 1) * 512],
                    [f"ytTd{go}", "ytg0_h0", "ytg0_h1"],
                    ["ytg0_h0", "ytg0_h1"], tag="qylda")
                dma("sync", ytg2[:],
                    ytT_d.ap().rearrange("(c p) s -> p c s", p=128)[
                        :, :, T + go * 512:T + (go + 1) * 512],
                    [f"ytTd{4 + go}", "ytg2"], ["ytg2"], tag="qyldb")
                A("vector", lambda e: e.tensor_scalar_mul(
                    yos[:], ytg[0][:], sel0[:, 0:1]),
                  reads=["ytg0_h0", "ytg0_h1", "sel0", "yos"],
                  writes=["yos"], tag="sel0")
                A("vector", lambda e: e.scalar_tensor_tensor(
                    yos[:], ytg2[:], sel1[:, 0:1], yos[:], ALU.mult, ALU.add),
                  reads=["ytg2", "sel1"], writes=["yos"], tag="sel1")
                for c in range(8):
                    A("scalar", lambda e, c=c: e.activation(
                        xnb[0][:, c * 512:(c + 1) * 512],
                        yos[:, c * 512:(c + 1) * 512], AF.Identity,
                        bias=lnp["ln1b"][:, c:c + 1],
                        scale=lnp["ln1g"][:, c:c + 1]),
                      reads=["yos", "ln1g", "ln1b", f"xnb0_{c}"],
                      writes=[f"xnb0_{c}"], tag="qxn")
                for fc in range(2):
                    pse, pk = (ps5a, P5A) if fc == 0 else (ps5b, P5B)
                    for c in range(8):
                        A("tensor", lambda e, c=c, fc=fc, pse=pse: e.matmul(
                            pse[:, 0:512],
                            wkh[:, c * 256 + fc * 128:c * 256 + (fc + 1) * 128],
                            xnb[0][:, c * 512:(c + 1) * 512],
                            start=(c == 0), stop=(c == 7)),
                          reads=[f"xnb0_{c}", "wkh", pk], writes=[pk],
                          tag="qmm")
                    A("vector", lambda e, fc=fc, go=go, pse=pse, hg=hg:
                      e.tensor_scalar(qT[:, fc * T + go * 512:
                                         fc * T + (go + 1) * 512],
                                      pse[:, 0:512], RSQ,
                                      bqs[:, hg * 2 + fc:hg * 2 + fc + 1],
                                      ALU.mult, ALU.add),
                      reads=[pk, "bqs", f"qT{fc}_{go}"],
                      writes=[f"qT{fc}_{go}", pk], tag="qev")

            for hl in range(2):
                h = hg * 2 + hl
                for part, nch, src_, srck in (
                        (0, 4, kT, "kT"), (2, 2, qT, "qT")):
                    for bch in range(nch):
                        A("scalar", lambda e, hl=hl, bch=bch, src_=src_,
                          part=part: e.activation(
                              expb[0][:],
                              src_[:, hl * (S if part == 0 else T) +
                                   bch * 1024:
                                   hl * (S if part == 0 else T) +
                                   (bch + 1) * 1024],
                              AF.Square),
                          reads=[f"{srck}{hl}_{2 * bch}",
                                 f"{srck}{hl}_{2 * bch + 1}", "expb0"],
                          writes=["expb0"], tag="sq")
                        for ch in range(2):
                            A("tensor", lambda e, ch=ch: e.matmul(
                                ps5a[0:1, 0:512], ones_col[:],
                                expb[0][:, ch * 512:(ch + 1) * 512],
                                start=True, stop=True),
                              reads=["expb0", "ones_col", P5A], writes=[P5A],
                              tag="nmm")
                            first = (bch == 0 and ch == 0)
                            A("vector", lambda e, part=part, first=first:
                              e.reduce_max(
                                  smr[0:1, part:part + 1] if first
                                  else smr[0:1, 1:2],
                                  ps5a[0:1, 0:512], axis=AX.X),
                              reads=[P5A, "smr"], writes=["smr", P5A],
                              tag="nmax")
                            if not first:
                                A("vector", lambda e, part=part: e.tensor_max(
                                    smr[0:1, part:part + 1],
                                    smr[0:1, part:part + 1], smr[0:1, 1:2]),
                                  reads=["smr"], writes=["smr"], tag="nacc")
                A("vector", lambda e: e.tensor_mul(
                    smr[0:1, 3:4], smr[0:1, 0:1], smr[0:1, 2:3]),
                  reads=["smr"], writes=["smr"], tag="mz")
                rsqrt_small(smr[0:1, 3:4], "smr", smr[0:1, 4:5],
                            smi[0:1, 0:1].bitcast(I32), smr[0:1, 5:6],
                            ["smr"])
                A("vector", lambda e: e.tensor_mul(
                    smr[0:1, 6:7], smr[0:1, 3:4], smr[0:1, 4:5]),
                  reads=["smr"], writes=["smr"], tag="msqrt")
                A("vector", lambda e: e.tensor_scalar(
                    smr[0:1, 6:7], smr[0:1, 6:7], 1.02, 0.01, ALU.mult,
                    ALU.add),
                  reads=["smr"], writes=["smr"], tag="mpad")
                A("vector", lambda e: e.tensor_scalar_min(
                    smr[0:1, 6:7], smr[0:1, 6:7], 30.0),
                  reads=["smr"], writes=["smr"], tag="mclip")
                A("vector", lambda e: e.tensor_copy(smb[0:1, 0:1],
                                                    smr[0:1, 6:7]),
                  reads=["smr"], writes=["smbk"], tag="mcast")
                A("tensor", lambda e: e.matmul(
                    ps5b[:, 0:1], ones_row[:], smb[0:1, 0:1],
                    start=True, stop=True),
                  reads=["smbk", "ones_row", P5B], writes=[P5B], tag="mbc")
                A("vector", lambda e, h=h: e.tensor_scalar_mul(
                    negm[:, h:h + 1], ps5b[:, 0:1], -1.0),
                  reads=[P5B, "negm"], writes=["negm", P5B], tag="mneg")

            scan_state["target"] = 1100.0
            for hl in range(2):
                h = hg * 2 + hl

                def emit_scores(pr, hl=hl, qg_ref=None):
                    pass

                for qg in range(4):
                    def scores_of(pr):
                        pse, pks = (psA, [PA0, PA1]) if pr % 2 == 0 \
                            else (psB, [PB0, PB1])
                        for u in range(2):
                            skc = pr * 2 + u
                            A("tensor",
                              lambda e, hl=hl, qg=qg, skc=skc, u=u, pse=pse:
                              e.matmul(pse[:, u * 512:(u + 1) * 512],
                                       kT[:, hl * S + skc * 128:
                                           hl * S + (skc + 1) * 128],
                                       qT[:, hl * T + qg * 512:
                                           hl * T + (qg + 1) * 512],
                                       start=True, stop=True),
                              reads=[f"kT{hl}_{skc // 4}",
                                     f"qT{hl}_{qg}", pks[u]],
                              writes=[pks[u]], tag="smm")

                    def exp_of(pr):
                        pse, pks = (psA, [PA0, PA1]) if pr % 2 == 0 \
                            else (psB, [PB0, PB1])
                        ek = pr % 3
                        A("scalar", lambda e, h=h, pse=pse, ek=ek:
                          e.activation(expb[ek][:], pse[:], AF.Exp,
                                       bias=negm[:, h:h + 1]),
                          reads=pks + ["negm", f"expb{ek}"],
                          writes=[f"expb{ek}"] + pks, tag="exp")

                    def ctx_of(pr):
                        ek = pr % 3
                        for u in range(2):
                            skc = pr * 2 + u
                            for qs in range(4):
                                cps = ps5a if qs < 2 else ps5b
                                cpk = P5A if qs < 2 else P5B
                                co = (qs % 2) * 129
                                A("tensor",
                                  lambda e, u=u, qs=qs, skc=skc, hl=hl,
                                  ek=ek, cps=cps, co=co:
                                  e.matmul(cps[:, co:co + 129],
                                           expb[ek][:, u * 512 + qs * 128:
                                                     u * 512 + (qs + 1) * 128],
                                           vsb[:, skc * 258 + hl * 129:
                                               skc * 258 + (hl + 1) * 129],
                                           start=(skc == 0 and qs % 2 == 0),
                                           stop=(skc == 31 and qs % 2 == 1),
                                           skip_group_check=True),
                                  reads=[f"expb{ek}", f"vsb{skc}", cpk],
                                  writes=[cpk], tag="cmm")

                    for pr in range(16):
                        scores_of(pr)
                        if pr > 1:
                            ctx_of(pr - 2)
                        exp_of(pr)
                    ctx_of(14)
                    ctx_of(15)
                    for qs in range(4):
                        cps = ps5a if qs < 2 else ps5b
                        cpk = P5A if qs < 2 else P5B
                        co = (qs % 2) * 129
                        A("vector", lambda e, cps=cps, co=co, qs=qs:
                          e.reciprocal(st1[qs][:, 0:1],
                                       cps[:, co + 128:co + 129]),
                          reads=[cpk, f"crec{qs}"], writes=[f"crec{qs}"],
                          tag="crec")
                        A("vector", lambda e, cps=cps, co=co, qs=qs:
                          e.tensor_scalar_mul(ctxe[:, qs * 128:(qs + 1) * 128],
                                              cps[:, co:co + 128],
                                              st1[qs][:, 0:1]),
                          reads=[cpk, f"crec{qs}", "ctxe"],
                          writes=["ctxe", cpk], tag="cev")
                        A("tensor", lambda e, qs=qs: e.matmul(
                            psT[:, qs * 128:(qs + 1) * 128],
                            ctxe[:, qs * 128:(qs + 1) * 128], ident[:],
                            start=True, stop=True),
                          reads=["ctxe", "ident", PT], writes=[PT], tag="ctr")
                    A("vector", lambda e: e.tensor_copy(ctxb[:],
                                                        psT[:, 0:512]),
                      reads=[PT, "ctxb"], writes=["ctxb", PT], tag="ctev")
                    dma("sync", ctxT_d.ap()[h * 128:(h + 1) * 128,
                                            qg * 512:(qg + 1) * 512],
                        ctxb[:], ["ctxb"], [f"ctxd{h}_{qg}"], tag="cspill")

        tap_copy("negm", negm[:], ["negm"])
        tap_copy("ctxT", ctxT_d.ap(),
                 [f"ctxd{h}_{qg}" for h in range(8) for qg in range(4)])

        if "hall" in taps:
            pump_scan(S)
            dma("sync", taps["hall"].ap(), hall[:],
                [f"h{t}" for t in range(S)], ["tap_hall"])

        st23b.close()
        st23.close()
        p.epoch("p4a", lambda e: e.memset(epb[0:1, 0:1], 0.0))

        scan_state["target"] = 600.0
        st4a = ExitStack()
        sb4a = lambda n, sh, dt: st4a.enter_context(
            nc.sbuf_tensor(n, list(sh), dt))
        wos = sb4a("woss", (128, 8 * 1024), BF16)
        wgs = sb4a("wgss", (128, 8 * 1024), BF16)
        ctb = [sb4a(f"ctbs{i}", (128, 1024), BF16) for i in range(2)]
        gtb = sb4a("gtbs", (128, 1024), BF16)
        gsc = sb4a("gscs", (128, 1024), BF16)
        zb4 = sb4a("zb4s", (128, 1024), F32)

        for hf in range(2):
            dma("gpsimd",
                wos[:, hf * 4096:(hf + 1) * 4096].rearrange(
                    "p (c f) -> p c f", c=4),
                dram["Wo"].ap()[hf * 512:(hf + 1) * 512, :].rearrange(
                    "(c p) f -> p c f", p=128),
                [], [f"wos_h{hf}"], tag="wo")
            dma("gpsimd",
                wgs[:, hf * 4096:(hf + 1) * 4096].rearrange(
                    "p (c f) -> p c f", c=4),
                dram["Wgate"].ap()[hf * 512:(hf + 1) * 512, :].rearrange(
                    "(c p) f -> p c f", p=128),
                [], [f"wgs_h{hf}"], tag="wg")

        for go in range(4):
            dma("sync", ytg[0][:],
                ytT_d.ap().rearrange("(c p) s -> p c s", p=128)[
                    :, :, go * 512:(go + 1) * 512],
                [f"ytTd{go}", "ytg0_h0", "ytg0_h1"],
                ["ytg0_h0", "ytg0_h1"], tag="gylda")
            dma("sync", ytg2[:],
                ytT_d.ap().rearrange("(c p) s -> p c s", p=128)[
                    :, :, T + go * 512:T + (go + 1) * 512],
                [f"ytTd{4 + go}", "ytg2"], ["ytg2"], tag="gyldb")
            A("vector", lambda e: e.tensor_scalar_mul(
                yos[:], ytg[0][:], sel0[:, 0:1]),
              reads=["ytg0_h0", "ytg0_h1", "sel0", "yos"],
              writes=["yos"], tag="gsel0")
            A("vector", lambda e: e.scalar_tensor_tensor(
                yos[:], ytg2[:], sel1[:, 0:1], yos[:], ALU.mult, ALU.add),
              reads=["ytg2", "sel1"], writes=["yos"], tag="gsel1")
            for c in range(8):
                A("scalar", lambda e, c=c: e.activation(
                    xnb[0][:, c * 512:(c + 1) * 512],
                    yos[:, c * 512:(c + 1) * 512], AF.Identity,
                    bias=lnp["lnb"][:, c:c + 1], scale=lnp["lng"][:, c:c + 1]),
                  reads=["yos", "lng", "lnb", f"xnb0_{c}"],
                  writes=[f"xnb0_{c}"], tag="gxn")

            for j in range(4):
                i = go * 4 + j
                bsel = i % 2
                xk = f"xb{bsel}"
                dma("sync", xb[bsel][:],
                    dram["x_own"].ap()[i * 128:(i + 1) * 128, :],
                    [xk], [xk], tag=f"p4x{i}")
                ck = f"ctb{bsel}"
                dma("gpsimd", ctb[bsel][:],
                    ctxT_d.ap().rearrange("(c p) t -> p c t", p=128)[
                        :, :, i * 128:(i + 1) * 128],
                    [ck] + [f"ctxd{h}_{i // 4}" for h in range(8)],
                    [ck], tag=f"p4c{i}")
                nmr, rstd, lk = ln_tile(xk, xb[bsel][:], i % 4, bsel)
                yk = f"yb{bsel}"
                A("vector", lambda e, b_=bsel, m=nmr, r=rstd: e.tensor_scalar(
                    yb[b_][:], xb[b_][:], r[:, 0:1], m[:, 0:1],
                    ALU.mult, ALU.add),
                  reads=[xk, lk], writes=[yk], tag="p4y")
                po = (i % 2) * 512
                for half in range(2):
                    pse = psA if half == 0 else psB
                    pk = (PA0 if half == 0 else PB0) if i % 2 == 0 \
                        else (PA1 if half == 0 else PB1)
                    for c in range(8):
                        A("tensor",
                          lambda e, b_=bsel, c=c, half=half, pse=pse, po=po:
                          e.matmul(pse[:, po:po + 512],
                                   ctb[b_][:, c * 128:(c + 1) * 128],
                                   wos[:, c * 1024 + half * 512:
                                       c * 1024 + (half + 1) * 512],
                                   start=(c == 0), stop=False),
                          reads=[ck, f"wos_h{c // 4}", pk], writes=[pk],
                          tag="womm")
                    A("tensor", lambda e, half=half, pse=pse, po=po:
                      e.matmul(
                        pse[:, po:po + 512], ones_row[:],
                        row_bf("bo", half * 512, 512), start=False, stop=True),
                      reads=["ones_row", "rows_b", pk], writes=[pk],
                      tag="wob")
                    A("vector", lambda e, half=half, pse=pse, b_=bsel, po=po:
                      e.tensor_add(zb4[:, half * 512:(half + 1) * 512],
                                   pse[:, po:po + 512],
                                   yb[b_][:, half * 512:(half + 1) * 512]),
                      reads=[pk, yk, "zb4"], writes=["zb4", pk], tag="x2")
                dma("sync", x2_d.ap()[i * 128:(i + 1) * 128, :], zb4[:],
                    ["zb4"], [f"x2d{i}"], tag="x2sp")
                if "x2" in taps:
                    dma("sync", taps["x2"].ap()[i * 128:(i + 1) * 128, :],
                        zb4[:], ["zb4"], ["tap_x2"])
                for half in range(2):
                    pse = psA if half == 0 else psB
                    pk = (PA0 if half == 0 else PB0) if i % 2 == 0 \
                        else (PA1 if half == 0 else PB1)
                    for c in range(8):
                        A("tensor",
                          lambda e, c=c, j=j, half=half, pse=pse, po=po:
                          e.matmul(pse[:, po:po + 512],
                                   xnb[0][:, c * 512 + j * 128:
                                       c * 512 + (j + 1) * 128],
                                   wgs[:, c * 1024 + half * 512:
                                       c * 1024 + (half + 1) * 512],
                                   start=(c == 0), stop=False),
                          reads=[f"xnb0_{c}", f"wgs_h{c // 4}", pk],
                          writes=[pk], tag="gmm")
                    A("tensor", lambda e, half=half, pse=pse, po=po:
                      e.matmul(
                        pse[:, po:po + 512], ones_row[:],
                        row_bf("bgate", half * 512, 512),
                        start=False, stop=True),
                      reads=["ones_row", "rows_b", pk], writes=[pk], tag="gb")
                    A("scalar", lambda e, half=half, pse=pse, po=po:
                      e.activation(
                        gsc[:, half * 512:(half + 1) * 512],
                        pse[:, po:po + 512],
                        AF.Tanh, scale=0.5),
                      reads=[pk, "gsc"], writes=["gsc", pk], tag="gth")
                A("vector", lambda e: e.tensor_scalar(
                    gtb[:], gsc[:], 0.5, 0.5, ALU.mult, ALU.add),
                  reads=["gsc", "gtb"], writes=["gtb"], tag="gfin")
                dma("sync", gate_d.ap()[i * 128:(i + 1) * 128, :],
                    gtb[:], ["gtb"], [f"gated{i}"], tag="gatesp")

        st4a.close()
        stR.close()
        p.epoch("p4b", lambda e: e.memset(epb[0:1, 0:1], 0.0))

        scan_state["target"] = 600.0
        st4b = ExitStack()
        sb4b = lambda n, sh, dt: st4b.enter_context(
            nc.sbuf_tensor(n, list(sh), dt))
        xn2T = sb4b("xn2Ts", (128, 8 * 1024), BF16)
        facc = sb4b("faccs", (128, 8 * 1024), F32)
        w1c = [sb4b("w1cs0", (128, 8 * 512), BF16)] * 2
        w2c = [sb4b(f"w2cs{i}", (128, 4 * 1024), BF16) for i in range(2)]
        h1b = [sb4b(f"h1bs{i}", (128, 4 * 1024), BF16) for i in range(2)]

        xbig4 = st4b.enter_context(
            nc.sbuf_tensor("xbig4s", [128, 4096], F32, side="right"))
        for th in range(2):
            for grp in range(2):
                for j in range(4):
                    i = th * 8 + grp * 4 + j
                    dma("sync", xbig4[:, j * 1024:(j + 1) * 1024],
                        x2_d.ap()[i * 128:(i + 1) * 128, :],
                        [f"xbig{j}", f"x2d{i}"], [f"xbig{j}"], tag="x2ld")
                nmrw, rstdw, lks, lco = ln_group(
                    [f"xbig{j}" for j in range(4)],
                    [xbig4[:, j * 1024:(j + 1) * 1024] for j in range(4)],
                    grp % 2)
                for j in range(4):
                    i8 = grp * 4 + j
                    i = th * 8 + i8
                    bsel = i8 % 2
                    A("vector", lambda e, b_=bsel, j=j, lco=lco:
                      e.tensor_scalar(
                        ybf[b_][:], xbig4[:, j * 1024:(j + 1) * 1024],
                        rstdw[:, lco + j:lco + j + 1],
                        nmrw[:, lco + j:lco + j + 1],
                        ALU.mult, ALU.add),
                      reads=[f"xbig{j}", f"ybf{bsel}"] + lks,
                      writes=[f"ybf{bsel}"], tag="x2h")
                    for r in range(2):
                        ps_, bkk, off = TRB3[(r + 2 * j) % 3]
                        for q in range(4):
                            c = r * 4 + q
                            A("tensor",
                              lambda e, b_=bsel, c=c, q=q, ps_=ps_, off=off:
                              e.matmul(
                                ps_[:, off + q * 128:off + (q + 1) * 128],
                                ybf[b_][:, c * 128:(c + 1) * 128], ident[:],
                                start=True, stop=True),
                              reads=[f"ybf{bsel}", "ident", bkk],
                              writes=[bkk], tag="x2tr")
                        for q in range(4):
                            c = r * 4 + q
                            A("vector",
                              lambda e, c=c, q=q, i8=i8, ps_=ps_, off=off,
                              bkk=bkk:
                              e.tensor_scalar(
                                  xn2T[:, c * 1024 + i8 * 128:
                                       c * 1024 + (i8 + 1) * 128],
                                  ps_[:, off + q * 128:off + (q + 1) * 128],
                                  lnp["ln2g"][:, c:c + 1],
                                  lnp["ln2b"][:, c:c + 1],
                                  ALU.mult, ALU.add),
                              reads=[bkk, "ln2g", "ln2b", "xn2T"],
                              writes=["xn2T"] + ([bkk] if q == 3 else []),
                              tag="xn2ev")
            for fc in range(8):
                fb = fc % 2
                w1k, w2k, h1k = "w1c0", f"w2c{fb}", f"h1b{fb}"
                for hf in range(2):
                    dma("gpsimd",
                        w1c[fb][:, hf * 2048:(hf + 1) * 2048].rearrange(
                            "p (c f) -> p c f", c=4),
                        dram["W1"].ap()[hf * 512:(hf + 1) * 512,
                                        fc * 512:(fc + 1) * 512].rearrange(
                            "(c p) f -> p c f", p=128),
                        [f"{w1k}_h{hf}"], [f"{w1k}_h{hf}"], tag="w1l")
                dma("gpsimd", w2c[fb][:].rearrange("p (c f) -> p c f", c=4),
                    dram["W2"].ap()[fc * 512:(fc + 1) * 512, :].rearrange(
                        "(c p) f -> p c f", p=128),
                    [w2k], [w2k], tag="w2l")
                for mc in range(4):
                    for tg in range(2):
                        pse, pk = (ps5a, P5A) if (mc * 2 + tg) % 2 == 0 \
                            else (ps5b, P5B)
                        for c in range(8):
                            A("tensor",
                              lambda e, c=c, mc=mc, tg=tg, fb=fb, pse=pse:
                              e.matmul(pse[:, 0:512],
                                       w1c[fb][:, c * 512 + mc * 128:
                                           c * 512 + (mc + 1) * 128],
                                       xn2T[:, c * 1024 + tg * 512:
                                            c * 1024 + (tg + 1) * 512],
                                       start=(c == 0), stop=(c == 7)),
                              reads=[f"{w1k}_h{c // 4}", "xn2T", pk],
                              writes=[pk], tag="w1mm")
                        A("vector",
                          lambda e, mc=mc, tg=tg, fb=fb, fc=fc, pse=pse:
                          e.tensor_scalar(h1b[fb][:, mc * 1024 + tg * 512:
                                                  mc * 1024 + (tg + 1) * 512],
                                          pse[:, 0:512],
                                          b1c[:, fc * 4 + mc:
                                              fc * 4 + mc + 1], 0.0,
                                          ALU.add, ALU.max),
                          reads=[pk, "b1c", h1k], writes=[h1k, pk],
                          tag="relu")
                for i8 in range(8):
                    po = (i8 % 2) * 512
                    for half in range(2):
                        pse = psA if half == 0 else psB
                        pk = (PA0 if half == 0 else PB0) if i8 % 2 == 0 \
                            else (PA1 if half == 0 else PB1)
                        for mc in range(4):
                            A("tensor",
                              lambda e, i8=i8, mc=mc, half=half, fb=fb,
                              pse=pse, fc=fc, po=po:
                              e.matmul(pse[:, po:po + 512],
                                       h1b[fb][:, mc * 1024 + i8 * 128:
                                           mc * 1024 + (i8 + 1) * 128],
                                       w2c[fb][:, mc * 1024 + half * 512:
                                           mc * 1024 + (half + 1) * 512],
                                       start=(mc == 0),
                                       stop=(mc == 3 and fc != 0)),
                              reads=[h1k, w2k, pk], writes=[pk], tag="w2mm")
                        if fc == 0:
                            A("tensor", lambda e, half=half, pse=pse, po=po:
                              e.matmul(pse[:, po:po + 512], ones_row[:],
                                       row_bf("b2", half * 512, 512),
                                       start=False, stop=True),
                              reads=["ones_row", "rows_b", pk], writes=[pk],
                              tag="b2b")
                            A("vector",
                              lambda e, i8=i8, half=half, pse=pse, po=po:
                              e.tensor_copy(
                                  facc[:, i8 * 1024 + half * 512:
                                       i8 * 1024 + (half + 1) * 512],
                                  pse[:, po:po + 512]),
                              reads=[pk, "facc"], writes=["facc", pk],
                              tag="facc0")
                        else:
                            A("vector",
                              lambda e, i8=i8, half=half, pse=pse, po=po:
                              e.tensor_add(
                                  facc[:, i8 * 1024 + half * 512:
                                       i8 * 1024 + (half + 1) * 512],
                                  facc[:, i8 * 1024 + half * 512:
                                       i8 * 1024 + (half + 1) * 512],
                                  pse[:, po:po + 512]),
                              reads=[pk, "facc"], writes=["facc", pk],
                              tag="facc")
            if "ffn" in taps and th == 0:
                tap_copy("ffn", facc[:], ["facc"])
            for i8 in range(8):
                i = th * 8 + i8
                bsel = i8 % 2
                xk = f"xb{bsel}"
                dma("sync", xb[bsel][:], x2_d.ap()[i * 128:(i + 1) * 128, :],
                    [xk, f"x2d{i}"], [xk], tag="x2ld2")
                A("vector", lambda e, b_=bsel, i8=i8: e.tensor_add(
                    yb[b_][:], xb[b_][:], facc[:, i8 * 1024:(i8 + 1) * 1024]),
                  reads=[xk, "facc", f"yb{bsel}"], writes=[f"yb{bsel}"],
                  tag="tout")
                A("vector", lambda e, b_=bsel: e.tensor_copy(ybf[b_][:],
                                                             yb[b_][:]),
                  reads=[f"yb{bsel}", f"ybf{bsel}"], writes=[f"ybf{bsel}"],
                  tag="tcast")
                transpose8(ybf[bsel][:], f"ybf{bsel}",
                           lambda r: tstage[:, 0:1024].rearrange(
                               "p (c t) -> p c t", c=8)[:, r * 4:(r + 1) * 4],
                           "tstage", banks=TRB3, evac="scalar")
                dma("sync",
                    moT_d.ap().rearrange("(c p) t -> p c t", p=128)[
                        :, 8:16, i * 128:(i + 1) * 128],
                    tstage[:, 0:1024].rearrange("p (c t) -> p c t", c=8),
                    ["tstage"], [f"moTd8{i}"], tag="tosp")

        st4b.close()

        st4bm = ExitStack()
        sb4bm = lambda n, sh, dt: st4bm.enter_context(
            nc.sbuf_tensor(n, list(sh), dt))
        gld = [sb4bm(f"glds{i}", (128, 1024), BF16) for i in range(2)]
        mob = sb4bm("mobs", (128, 1024), F32)
        for i in range(16):
            bsel = i % 2
            xk = f"xb{bsel}"
            gk = f"gld{bsel}"
            dma("sync", xb[bsel][:],
                dram["x_own"].ap()[i * 128:(i + 1) * 128, :], [xk], [xk],
                tag="bmx")
            dma("sync", gld[bsel][:], gate_d.ap()[i * 128:(i + 1) * 128, :],
                [gk, f"gated{i}"], [gk], tag="bmg")
            nmr, rstd, lk = ln_tile(xk, xb[bsel][:], i % 4, bsel)
            yk = f"yb{bsel}"
            A("vector", lambda e, b_=bsel, m=nmr, r=rstd: e.tensor_scalar(
                yb[b_][:], xb[b_][:], r[:, 0:1], m[:, 0:1],
                ALU.mult, ALU.add),
              reads=[xk, lk, yk], writes=[yk], tag="bmy")
            pump_scan(max(0, T + (i + 1) * 128 - scan_state["t"]))
            A("vector", lambda e, i=i: e.tensor_scalar_mul(
                hown[:, i * 128:(i + 1) * 128],
                hall[:, i * 128:(i + 1) * 128], sel0[0:16, 0:1]),
              reads=[f"h{i * 128 + 127}", "sel0", "hown"],
              writes=["hown"], tag="hsel0")
            A("vector", lambda e, i=i: e.scalar_tensor_tensor(
                hown[:, i * 128:(i + 1) * 128],
                hall[:, T + i * 128:T + (i + 1) * 128],
                sel1[0:16, 0:1], hown[:, i * 128:(i + 1) * 128],
                ALU.mult, ALU.add),
              reads=[f"h{T + i * 128 + 127}", "sel1", "hown"],
              writes=["hown"], tag="hsel1")
            for half in range(2):
                pse = psA if half == 0 else psB
                pk = PA0 if half == 0 else PB0
                A("tensor", lambda e, i=i, half=half, pse=pse: e.matmul(
                    pse[:, 0:512], hown[:, i * 128:(i + 1) * 128],
                    C_bf[:, half * 512:(half + 1) * 512],
                    start=True, stop=True),
                  reads=["hown", "C_bf", pk], writes=[pk], tag="hC")
                A("vector", lambda e, half=half, pse=pse, b_=bsel:
                  e.tensor_mul(
                      mob[:, half * 512:(half + 1) * 512], pse[:, 0:512],
                      gld[b_][:, half * 512:(half + 1) * 512]),
                  reads=[pk, gk, "mob"], writes=["mob", pk], tag="momul")
            A("vector", lambda e, b_=bsel: e.tensor_add(
                mob[:], mob[:], yb[b_][:]),
              reads=["mob", yk], writes=["mob"], tag="moadd")
            A("scalar", lambda e, b_=bsel: e.activation(
                ybf[b_][:], mob[:], AF.Identity),
              reads=["mob", f"ybf{bsel}"], writes=[f"ybf{bsel}"],
              tag="mocast")
            transpose8(ybf[bsel][:], f"ybf{bsel}",
                       lambda r: tstage[:, 0:1024].rearrange(
                           "p (c t) -> p c t", c=8)[:, r * 4:(r + 1) * 4],
                       "tstage", banks=TRB3, evac="scalar")
            dma("sync",
                moT_d.ap().rearrange("(c p) t -> p c t", p=128)[
                    :, 0:8, i * 128:(i + 1) * 128],
                tstage[:, 0:1024].rearrange("p (c t) -> p c t", c=8),
                ["tstage"], [f"moTd{i}"], tag="mosp")
        st4bm.close()
        stRs.close()
        p.epoch("p4c", lambda e: e.memset(epb[0:1, 0:1], 0.0))

        st4c = ExitStack()
        sb4c = lambda n, sh, dt: st4c.enter_context(
            nc.sbuf_tensor(n, list(sh), dt))
        wfs = sb4c("wfss", (128, 16 * 1024), BF16)
        lnfg = sb4c("lnfgs", (128, 1024), F32)
        lnfb = sb4c("lnfbs", (128, 1024), F32)
        mtb = [sb4c(f"mtbs{i}", (128, 2048), BF16) for i in range(2)]
        zb4c = sb4c("zb4cs", (128, 1024), F32)
        dma("gpsimd", wfs[:].rearrange("p (c f) -> p c f", c=16),
            dram["Wf"].ap().rearrange("(c p) f -> p c f", p=128),
            [], ["wfs"], tag="wf")
        dma("sync", lnfg[:], dram["lnfg_rep"].ap(), [], ["lnfg"])
        dma("sync", lnfb[:], dram["lnfb_rep"].ap(), [], ["lnfb"])

        zbig = st4c.enter_context(
            nc.sbuf_tensor("zbigs", [128, 4096], F32))
        for grp in range(4):
            for j in range(4):
                i = grp * 4 + j
                bsel = i % 2
                mk = f"mtb{bsel}"
                dma("sync", mtb[bsel][:],
                    moT_d.ap().rearrange("(c p) t -> p c t", p=128)[
                        :, :, i * 128:(i + 1) * 128],
                    [mk, f"moTd{i}", f"moTd8{i}"], [mk], tag="mold")
                for half in range(2):
                    pse = psA if half == 0 else psB
                    pk = (PA0 if half == 0 else PB0) if i % 2 == 0 \
                        else (PA1 if half == 0 else PB1)
                    po = (i % 2) * 512
                    for c in range(16):
                        A("tensor",
                          lambda e, c=c, half=half, pse=pse, b_=bsel, po=po:
                          e.matmul(pse[:, po:po + 512],
                                   mtb[b_][:, c * 128:(c + 1) * 128],
                                   wfs[:, c * 1024 + half * 512:
                                       c * 1024 + (half + 1) * 512],
                                   start=(c == 0), stop=False),
                          reads=[mk, "wfs", pk], writes=[pk], tag="wfmm")
                    A("tensor", lambda e, half=half, pse=pse, po=po:
                      e.matmul(
                        pse[:, po:po + 512], ones_row[:],
                        row_bf("bf", half * 512, 512),
                        start=False, stop=True),
                      reads=["ones_row", "rows_b", pk], writes=[pk],
                      tag="bfb")
                    A("scalar",
                      lambda e, half=half, pse=pse, po=po, j=j:
                      e.activation(
                          zbig[:, j * 1024 + half * 512:
                               j * 1024 + (half + 1) * 512],
                          pse[:, po:po + 512], AF.Identity),
                      reads=[pk, f"zbig{j}"], writes=[f"zbig{j}", pk],
                      tag="fev")
            nmrw, rstdw, lks, lco = ln_group(
                [f"zbig{j}" for j in range(4)],
                [zbig[:, j * 1024:(j + 1) * 1024] for j in range(4)],
                grp % 2)
            for j in range(4):
                i = grp * 4 + j
                bsel = i % 2
                xk = f"xb{bsel}"
                A("vector", lambda e, b_=bsel, j=j, lco=lco: e.tensor_scalar(
                    yb[b_][:], zbig[:, j * 1024:(j + 1) * 1024],
                    rstdw[:, lco + j:lco + j + 1],
                    nmrw[:, lco + j:lco + j + 1], ALU.mult, ALU.add),
                  reads=[f"zbig{j}", f"yb{bsel}"] + lks,
                  writes=[f"yb{bsel}"], tag="fnorm")
                dma("sync", xb[bsel][:],
                    dram["x_own"].ap()[i * 128:(i + 1) * 128, :], [xk], [xk],
                    tag="p4cx")
                A("vector", lambda e, b_=bsel: e.tensor_mul(yb[b_][:],
                                                            yb[b_][:],
                                                            lnfg[:]),
                  reads=[f"yb{bsel}", "lnfg"], writes=[f"yb{bsel}"], tag="fg")
                A("vector", lambda e, b_=bsel: e.tensor_add(yb[b_][:],
                                                            yb[b_][:],
                                                            lnfb[:]),
                  reads=[f"yb{bsel}", "lnfb"], writes=[f"yb{bsel}"],
                  tag="fbias")
                A("vector", lambda e, b_=bsel, j=j: e.tensor_add(
                    zbig[:, j * 1024:(j + 1) * 1024], yb[b_][:], xb[b_][:]),
                  reads=[f"yb{bsel}", xk, f"zbig{j}"], writes=[f"zbig{j}"],
                  tag="o1")
                dma("sync", out1_d.ap()[i * 128:(i + 1) * 128, :],
                    zbig[:, j * 1024:(j + 1) * 1024],
                    [f"zbig{j}"], [f"o1d{i}"], tag="o1sp")
                if "out1" in taps:
                    dma("sync", taps["out1"].ap()[i * 128:(i + 1) * 128, :],
                        zbig[:, j * 1024:(j + 1) * 1024], [f"zbig{j}"],
                        ["tap_out1"])
            nmr2, rstd2, lks2, lco2 = ln_group(
                [f"zbig{j}" for j in range(4)],
                [zbig[:, j * 1024:(j + 1) * 1024] for j in range(4)],
                (grp + 1) % 2)
            for j in range(4):
                i = grp * 4 + j
                bsel = i % 2
                A("vector", lambda e, b_=bsel, j=j, lco2=lco2:
                  e.tensor_scalar(
                    ybf[b_][:], zbig[:, j * 1024:(j + 1) * 1024],
                    rstd2[:, lco2 + j:lco2 + j + 1],
                    nmr2[:, lco2 + j:lco2 + j + 1], ALU.mult, ALU.add),
                  reads=[f"zbig{j}", f"ybf{bsel}"] + lks2,
                  writes=[f"ybf{bsel}"], tag="y2h")
                transpose8(ybf[bsel][:], f"ybf{bsel}",
                           lambda r: tstage[:, 0:1024].rearrange(
                               "p (c t) -> p c t", c=8)[:, r * 4:(r + 1) * 4],
                           "tstage", banks=TRB3, evac="scalar")
                dma("sync",
                    y2T_d.ap().rearrange("(c p) t -> p c t", p=128)[
                        :, :, i * 128:(i + 1) * 128],
                    tstage[:, 0:1024].rearrange("p (c t) -> p c t", c=8),
                    ["tstage"], [f"y2Td{i}"], tag="y2sp")
        tap_copy("y2T", y2T_d.ap(), [f"y2Td{i}" for i in range(16)])

        st4c.close()
        p.epoch("p5", lambda e: e.memset(epb[0:1, 0:1], 0.0))

        st5 = ExitStack()
        sb5 = lambda n, sh, dt: st5.enter_context(
            nc.sbuf_tensor(n, list(sh), dt))
        y2h = sb5("y2hs", (128, 8 * 1024), BF16)
        acc = sb5("accs", (128, 8 * 1024), F32)
        glub = sb5("glubs", (128, 16 * 1024), BF16)
        w1e = [sb5(f"w1es{i}", (128, 8 * 512), BF16) for i in range(2)]
        w2e = [sb5(f"w2es{i}", (128, 8 * 512), BF16) for i in range(2)]
        w3e = sb5("w3es", (128, 16 * 1024), BF16)
        s1b = sb5("s1bs", (128, 1024), BF16)
        pr8 = sb5("pr8s", (128, 8), F32)
        wtse = sb5("wtses", (128, 16), BF16)
        wtsT = sb5("wtsTs", (16, 128), BF16)
        b3all = sb5("b3alls", (16, 1024), BF16)

        dma("gpsimd", b3all[0:8, :], dram["rb3"].ap(),
            ["b3all"], ["b3all"], tag="b3la")
        dma("gpsimd", b3all[8:10, :], dram["sb3"].ap(),
            ["b3all"], ["b3all"], tag="b3lb")
        A("vector", lambda e: e.memset(wtse[:, 8:10], 0.5),
          reads=["wtse"], writes=["wtse8"], tag="wtsm")

        for th in range(2):
            for i8 in range(8):
                dma("sync",
                    y2h[:].rearrange("p (c t) -> p c t", c=8)[
                        :, :, i8 * 128:(i8 + 1) * 128],
                    y2T_d.ap().rearrange("(c p) t -> p c t", p=128)[
                        :, :, th * 1024 + i8 * 128:
                        th * 1024 + (i8 + 1) * 128],
                    [f"y2Td{th * 8 + i8}", f"y2h{i8}"],
                    [f"y2h{i8}"], tag="y2ld")
            def emit_gate(i8, th=th):
                gi = th * 8 + i8
                for c in range(8):
                    A("tensor", lambda e, c=c, i8=i8: e.matmul(
                        ps5a[:, 0:8],
                        y2h[:, c * 1024 + i8 * 128:c * 1024 + (i8 + 1) * 128],
                        gatew[:, c * 8:(c + 1) * 8],
                        start=(c == 0), stop=False),
                      reads=[f"y2h{i8}", "gatew", P5A], writes=[P5A],
                      tag="gsc")
                A("tensor", lambda e: e.matmul(
                    ps5a[:, 0:8], ones_row[:], row_bf("gateb", 0, 8),
                    start=False, stop=True),
                  reads=["ones_row", "rows_b", P5A], writes=[P5A], tag="gscb")
                A("vector", lambda e: e.reduce_max(st1[0][:, 0:1],
                                                   ps5a[:, 0:8],
                                                   axis=AX.X, negate=True),
                  reads=[P5A, "LNS0"], writes=["LNS0"], tag="gmax")
                A("scalar", lambda e: e.activation(pr8[:], ps5a[:, 0:8],
                                                   AF.Exp,
                                                   bias=st1[0][:, 0:1],
                                                   accum_out=st2[0][:, 0:1]),
                  reads=[P5A, "LNS0", "pr8"], writes=["pr8", "LNS0", P5A],
                  tag="gexp")
                A("vector", lambda e: e.reciprocal(st2[0][:, 0:1],
                                                   st2[0][:, 0:1]),
                  reads=["LNS0"], writes=["LNS0"], tag="grec")
                A("vector", lambda e: e.tensor_scalar_mul(
                    pr8[:], pr8[:], st2[0][:, 0:1]),
                  reads=["pr8", "LNS0"], writes=["pr8"], tag="gp")
                A("vector", lambda e: e.reduce_max(st1[0][:, 0:1], pr8[:],
                                                   axis=AX.X),
                  reads=["pr8", "LNS0"], writes=["LNS0"], tag="gm1")
                A("vector", lambda e: e.tensor_scalar(
                    dump[0][:, 0:8], pr8[:], st1[0][:, 0:1], None, ALU.is_ge),
                  reads=["pr8", "LNS0", "dmp0"], writes=["dmp0"], tag="gis1")
                A("vector", lambda e: e.scalar_tensor_tensor(
                    yb[0][:, 0:8], dump[0][:, 0:8], -2.0, pr8[:], ALU.mult,
                    ALU.add),
                  reads=["dmp0", "pr8", "yb0"], writes=["yb0"], tag="gp2")
                A("vector", lambda e: e.reduce_max(st1[0][:, 0:1],
                                                   yb[0][:, 0:8], axis=AX.X),
                  reads=["yb0", "LNS0"], writes=["LNS0"], tag="gm2")
                A("vector", lambda e: e.tensor_scalar(
                    dump[0][:, 0:8], pr8[:], st1[0][:, 0:1], None, ALU.is_ge),
                  reads=["pr8", "LNS0", "dmp0"], writes=["dmp0"], tag="gis2")
                A("vector", lambda e, gi=gi: e.tensor_mul(
                    wts[:, gi * 8:(gi + 1) * 8], pr8[:], dump[0][:, 0:8]),
                  reads=["pr8", "dmp0", "wts"], writes=["wts"], tag="gwts")

            dma("sync", acc[:].rearrange("p (i d) -> p i d", i=8),
                out1_d.ap().rearrange("(i p) d -> p i d", p=128)[
                    :, th * 8:(th + 1) * 8, :],
                [f"o1d{i}" for i in range(th * 8, th * 8 + 8)] + ["acc"],
                ["acc"], tag="acci")
            pending_gates = list(range(8))
            for ex in range(10):
                routed = ex < 8
                wsrc1 = dram["rw1"] if routed else dram["sw1"]
                wsrc2 = dram["rw2"] if routed else dram["sw2"]
                wsrc3 = dram["rw3"] if routed else dram["sw3"]
                bsrc3 = dram["rb3"] if routed else dram["sb3"]
                eidx = ex if routed else ex - 8
                bias1 = rb1c if routed else sb1c
                bias2 = rb2c if routed else sb2c
                for hc in range(2):
                    wb = hc % 2
                    dma("gpsimd", w1e[wb][:].rearrange("p (c f) -> p c f", c=8),
                        wsrc1.ap()[eidx][:, hc * 512:(hc + 1) * 512]
                        .rearrange("(c p) f -> p c f", p=128),
                        [f"w1e{wb}"], [f"w1e{wb}"], tag="w1e")
                    dma("gpsimd", w2e[wb][:].rearrange("p (c f) -> p c f", c=8),
                        wsrc2.ap()[eidx][:, hc * 512:(hc + 1) * 512]
                        .rearrange("(c p) f -> p c f", p=128),
                        [f"w2e{wb}"], [f"w2e{wb}"], tag="w2e")
                dma("gpsimd", w3e[:].rearrange("p (c f) -> p c f", c=16),
                    wsrc3.ap()[eidx].rearrange("(c p) f -> p c f", p=128),
                    ["w3e"], ["w3e"], tag="w3l")
                for hc in range(4):
                    wb = hc % 2
                    w1k, w2k = f"w1e{wb}", f"w2e{wb}"
                    if hc >= 2:
                        dma("gpsimd", w1e[wb][:].rearrange("p (c f) -> p c f", c=8),
                            wsrc1.ap()[eidx][:, hc * 512:(hc + 1) * 512]
                            .rearrange("(c p) f -> p c f", p=128),
                            [w1k], [w1k], tag="w1e")
                        dma("gpsimd", w2e[wb][:].rearrange("p (c f) -> p c f", c=8),
                            wsrc2.ap()[eidx][:, hc * 512:(hc + 1) * 512]
                            .rearrange("(c p) f -> p c f", p=128),
                            [w2k], [w2k], tag="w2e")
                    for mc in range(4):
                        mcg = hc * 4 + mc
                        for tg in range(2):
                            par = (mc * 2 + tg) % 2
                            po = par * 512
                            pka = PA0 if par == 0 else PA1
                            pkb = PB0 if par == 0 else PB1
                            for c in range(8):
                                A("tensor",
                                  lambda e, c=c, mc=mc, tg=tg, wb=wb, po=po:
                                  e.matmul(psA[:, po:po + 512],
                                           w1e[wb][:, c * 512 + mc * 128:
                                               c * 512 + (mc + 1) * 128],
                                           y2h[:, c * 1024 + tg * 512:
                                               c * 1024 + (tg + 1) * 512],
                                           start=(c == 0), stop=(c == 7)),
                                  reads=[f"y2h{tg * 4}", f"y2h{tg * 4 + 1}",
                                         f"y2h{tg * 4 + 2}",
                                         f"y2h{tg * 4 + 3}", w1k, pka],
                                  writes=[pka], tag="p1mm")
                            for c in range(8):
                                A("tensor",
                                  lambda e, c=c, mc=mc, tg=tg, wb=wb, po=po:
                                  e.matmul(psB[:, po:po + 512],
                                           w2e[wb][:, c * 512 + mc * 128:
                                               c * 512 + (mc + 1) * 128],
                                           y2h[:, c * 1024 + tg * 512:
                                               c * 1024 + (tg + 1) * 512],
                                           start=(c == 0), stop=(c == 7)),
                                  reads=[f"y2h{tg * 4}", f"y2h{tg * 4 + 1}",
                                         f"y2h{tg * 4 + 2}",
                                         f"y2h{tg * 4 + 3}", w2k, pkb],
                                  writes=[pkb], tag="p2mm")
                            A("scalar",
                              lambda e, mcg=mcg, tg=tg, eidx=eidx, b_=bias1,
                              po=po:
                              e.activation(s1b[:, tg * 512:(tg + 1) * 512],
                                           psA[:, po:po + 512], AF.Silu,
                                           bias=b_[:, eidx * 16 + mcg:
                                                   eidx * 16 + mcg + 1]),
                              reads=[pka, "s1b"], writes=["s1b", pka],
                              tag="silu")
                            A("vector",
                              lambda e, mcg=mcg, tg=tg, eidx=eidx, b_=bias2,
                              po=po:
                              e.scalar_tensor_tensor(
                                  glub[:, mcg * 1024 + tg * 512:
                                       mcg * 1024 + (tg + 1) * 512],
                                  psB[:, po:po + 512],
                                  b_[:, eidx * 16 + mcg:eidx * 16 + mcg + 1],
                                  s1b[:, tg * 512:(tg + 1) * 512],
                                  ALU.add, ALU.mult),
                              reads=[pkb, "s1b", "glub"],
                              writes=["glub", pkb], tag="glu")
                            if ex == 0 and (mc * 2 + tg) % 2 == 1 \
                                    and pending_gates:
                                emit_gate(pending_gates.pop(0))
                for i8 in range(8):
                    gi8 = th * 8 + i8
                    if ex == 0:
                        A("vector", lambda e, gi8=gi8: e.tensor_copy(
                            wtse[:, 0:8], wts[:, gi8 * 8:(gi8 + 1) * 8]),
                          reads=["wts", "wtse"], writes=["wtse"],
                          tag="wtsec")
                        A("tensor", lambda e: e.matmul(
                            psT[0:10, 0:128], wtse[:, 0:10], ident[:],
                            start=True, stop=True),
                          reads=["wtse", "wtse8", "ident", PT],
                          writes=[PT], tag="wtr")
                        A("vector", lambda e: e.tensor_copy(
                            wtsT[0:10, :], psT[0:10, 0:128]),
                          reads=[PT, "wtsT"], writes=["wtsT", PT],
                          tag="wtev")
                        for hf2 in range(2):
                            A("tensor", lambda e, hf2=hf2: e.matmul(
                                psT[:, 0:512], wtsT[0:10, :],
                                b3all[0:10, hf2 * 512:(hf2 + 1) * 512],
                                start=True, stop=True),
                              reads=["wtsT", "b3all", PT], writes=[PT],
                              tag="b3pre")
                            A("vector", lambda e, i8=i8, hf2=hf2:
                              e.tensor_add(
                                  acc[:, i8 * 1024 + hf2 * 512:
                                      i8 * 1024 + (hf2 + 1) * 512],
                                  acc[:, i8 * 1024 + hf2 * 512:
                                      i8 * 1024 + (hf2 + 1) * 512],
                                  psT[:, 0:512]),
                              reads=[PT, "acc"], writes=["acc", PT],
                              tag="b3acc")
                    for hf in range(2):
                        pse, pk = (ps5a, P5A) if hf == 0 else (ps5b, P5B)
                        for c in range(16):
                            A("tensor", lambda e, c=c, i8=i8, hf=hf, pse=pse:
                              e.matmul(pse[:, 0:512],
                                       glub[:, c * 1024 + i8 * 128:
                                            c * 1024 + (i8 + 1) * 128],
                                       w3e[:, c * 1024 + hf * 512:
                                           c * 1024 + (hf + 1) * 512],
                                       start=(c == 0), stop=(c == 15)),
                              reads=["glub", "w3e", pk], writes=[pk],
                              tag="w3mm")
                        gi = th * 8 + i8
                        if routed:
                            A("vector",
                              lambda e, i8=i8, hf=hf, gi=gi, ex=ex, pse=pse:
                              e.scalar_tensor_tensor(
                                  acc[:, i8 * 1024 + hf * 512:
                                      i8 * 1024 + (hf + 1) * 512],
                                  pse[:, 0:512],
                                  wts[:, gi * 8 + ex:gi * 8 + ex + 1],
                                  acc[:, i8 * 1024 + hf * 512:
                                      i8 * 1024 + (hf + 1) * 512],
                                  ALU.mult, ALU.add),
                              reads=[pk, "wts", "acc"], writes=["acc", pk],
                              tag="accr")
                        else:
                            A("vector", lambda e, i8=i8, hf=hf, pse=pse:
                              e.scalar_tensor_tensor(
                                  acc[:, i8 * 1024 + hf * 512:
                                      i8 * 1024 + (hf + 1) * 512],
                                  pse[:, 0:512], 0.5,
                                  acc[:, i8 * 1024 + hf * 512:
                                      i8 * 1024 + (hf + 1) * 512],
                                  ALU.mult, ALU.add),
                              reads=[pk, "acc"], writes=["acc", pk],
                              tag="accs")
            for i8 in range(8):
                gi = th * 8 + i8
                dma("scalar", out_h.ap()[gi * 128:(gi + 1) * 128, :],
                    acc[:, i8 * 1024:(i8 + 1) * 1024],
                    ["acc"], [f"outd{gi}"], tag="outw")
        tap_copy("wts", wts[:], ["wts"])
        st5.close()

        p.barrier_all("sync")
        p.emit(nc, st0)

    return nc, taps


_CACHE = {}


def _prep_core_inputs(inputs, b, h):
    f32 = np.float32

    def asf(a):
        return np.ascontiguousarray(a, dtype=f32)

    def chunks(v, nc_):
        return np.ascontiguousarray(np.asarray(v, f32).reshape(nc_, 128).T)

    x = np.asarray(inputs["L0_attn_x"], f32)
    im = {}
    im["x_full"] = asf(x[b])
    im["x_own"] = asf(x[b, h * T:(h + 1) * T])
    im["sel0"] = np.full((128, 1), 1.0 if h == 0 else 0.0, f32)
    im["sel1"] = np.full((128, 1), 0.0 if h == 0 else 1.0, f32)
    im["ident"] = np.eye(128, dtype=BF16NP)
    im["ones_row"] = np.ones((1, 128), dtype=BF16NP)
    im["lng_c"] = chunks(inputs["L0_attn_ln_g"], 8)
    im["lnb_c"] = chunks(inputs["L0_attn_ln_b"], 8)
    im["ln1g_c"] = chunks(inputs["L0_attn_ln1_g"], 8)
    im["ln1b_c"] = chunks(inputs["L0_attn_ln1_b"], 8)
    im["ln2g_c"] = chunks(inputs["L0_attn_ln2_g"], 8)
    im["ln2b_c"] = chunks(inputs["L0_attn_ln2_b"], 8)
    im["bq_c"] = chunks(inputs["L0_attn_bq"], 8)
    im["bk_c"] = chunks(inputs["L0_attn_bk"], 8)
    im["b1_c"] = chunks(inputs["L0_attn_b1"], 32)
    im["rb1_c"] = np.ascontiguousarray(
        np.asarray(inputs["L0_ffn_rb1"], f32).reshape(8, 16, 128)
        .transpose(2, 0, 1).reshape(128, 128))
    im["rb2_c"] = np.ascontiguousarray(
        np.asarray(inputs["L0_ffn_rb2"], f32).reshape(8, 16, 128)
        .transpose(2, 0, 1).reshape(128, 128))
    im["sb1_c"] = np.ascontiguousarray(
        np.asarray(inputs["L0_ffn_sb1"], f32).reshape(2, 16, 128)
        .transpose(2, 0, 1).reshape(128, 32))
    im["sb2_c"] = np.ascontiguousarray(
        np.asarray(inputs["L0_ffn_sb2"], f32).reshape(2, 16, 128)
        .transpose(2, 0, 1).reshape(128, 32))
    im["lnfg_rep"] = np.ascontiguousarray(np.broadcast_to(
        np.asarray(inputs["L0_attn_lnf_g"], f32), (128, 1024)))
    im["lnfb_rep"] = np.ascontiguousarray(np.broadcast_to(
        np.asarray(inputs["L0_attn_lnf_b"], f32), (128, 1024)))
    im["bgate_row"] = asf(inputs["L0_attn_bgate"]).reshape(1, -1)
    im["bv_row"] = asf(inputs["L0_attn_bv"]).reshape(1, -1)
    im["bo_row"] = asf(inputs["L0_attn_bo"]).reshape(1, -1)
    im["b2_row"] = asf(inputs["L0_attn_b2"]).reshape(1, -1)
    im["bf_row"] = asf(inputs["L0_attn_bf"]).reshape(1, -1)
    im["rb3"] = asf(inputs["L0_ffn_rb3"])
    im["sb3"] = asf(inputs["L0_ffn_sb3"])
    im["gateb_row"] = asf(inputs["L0_ffn_gate_b"]).reshape(1, -1)
    im["A_f"] = asf(inputs["L0_attn_A"])
    im["bdelta_c"] = asf(inputs["L0_attn_bdelta"]).reshape(16, 1)
    im["bin_c"] = asf(inputs["L0_attn_b_in"]).reshape(16, 1)
    im["wdelta_r"] = np.ascontiguousarray(
        asf(inputs["L0_attn_Wdelta"]).reshape(8, 128, 16)
        .transpose(1, 0, 2).reshape(128, 128))
    im["win_r"] = np.ascontiguousarray(
        asf(inputs["L0_attn_Win"]).reshape(8, 128, 16)
        .transpose(1, 0, 2).reshape(128, 128))
    im["gatew_r"] = np.ascontiguousarray(
        asf(inputs["L0_ffn_gate_w"]).reshape(8, 128, 8)
        .transpose(1, 0, 2).reshape(128, 64))
    im["C_w"] = asf(inputs["L0_attn_C"])
    for nm, key in (("Wq", "L0_attn_Wq"), ("Wk", "L0_attn_Wk"),
                    ("Wv", "L0_attn_Wv"), ("Wo", "L0_attn_Wo"),
                    ("Wgate", "L0_attn_Wgate"), ("W1", "L0_attn_W1"),
                    ("W2", "L0_attn_W2"), ("Wf", "L0_attn_Wf"),
                    ("rw1", "L0_ffn_rw1"), ("rw2", "L0_ffn_rw2"),
                    ("rw3", "L0_ffn_rw3"), ("sw1", "L0_ffn_sw1"),
                    ("sw2", "L0_ffn_sw2"), ("sw3", "L0_ffn_sw3")):
        im[nm] = asf(inputs[key])
    return im


def kernel(**inputs):
    if "nc" not in _CACHE:
        _CACHE["nc"] = build_nc()
    nc, _taps = _CACHE["nc"]
    in_maps = [_prep_core_inputs(inputs, c // 2, c % 2) for c in range(8)]
    res = run_bass_kernel_spmd(nc, in_maps, core_ids=list(range(8)))
    out = np.empty((B, S, D), np.float32)
    for c in range(8):
        out[c // 2, (c % 2) * T:(c % 2 + 1) * T, :] = np.asarray(
            res.results[c]["out"], np.float32)
    return out


# version 20
# speedup vs baseline: 1.0470x; 1.0470x over previous
import numpy as np
import ml_dtypes
from contextlib import ExitStack

import concourse.bass as bass
import concourse.mybir as mybir
from concourse.bass_utils import run_bass_kernel_spmd

F32 = mybir.dt.float32
BF16 = mybir.dt.bfloat16
I32 = mybir.dt.int32
AF = mybir.ActivationFunctionType
ALU = mybir.AluOpType
AX = mybir.AxisListType
BF16NP = ml_dtypes.bfloat16

TAPS = ()

B, S, D = 4, 4096, 1024
T = S // 2
DFF = 4096
HE = 2048
NEXP = 8
RSQ = float(1.0 / np.sqrt(128.0))
MAGIC = float(0x5F3759DF)

ENGINES = ("tensor", "scalar", "vector", "gpsimd", "sync")
DMA_QUEUES = ("sync", "gpsimd", "scalar")
N_SLOTS = {"sync": 12, "gpsimd": 12, "scalar": 4}


class Prog:

    def __init__(self):
        self.ops = []
        self.writer = {}
        self.readers = {}
        self.dma_count = {q: 0 for q in DMA_QUEUES}
        self.epoch_key = None

    def add(self, engine, fn, reads=(), writes=(), dma=False, tag=""):
        deps = set()
        rds = list(reads)
        if self.epoch_key:
            rds.append(self.epoch_key)
        for r in rds:
            w = self.writer.get(r)
            if w is not None:
                deps.add(w)
        for wk in writes:
            w = self.writer.get(wk)
            if w is not None:
                deps.add(w)
            for rd in self.readers.get(wk, ()):
                deps.add(rd)
        i = len(self.ops)
        op = dict(engine=engine, fn=fn, deps=sorted(deps), dma=dma, tag=tag,
                  carrier=False, inc_count=None, slot=None, slot_use=None)
        if dma:
            assert engine in DMA_QUEUES
            k = self.dma_count[engine]
            self.dma_count[engine] += 1
            op["slot"] = (engine, k % N_SLOTS[engine])
            op["slot_use"] = k // N_SLOTS[engine] + 1
        self.ops.append(op)
        for wk in writes:
            self.writer[wk] = i
            self.readers[wk] = []
        for r in rds:
            self.readers.setdefault(r, []).append(i)
        return i

    def _fence_deps(self):
        last = {}
        for i, op in enumerate(self.ops):
            last[(op["engine"], op["slot"])] = i
        return sorted(set(last.values()))

    def epoch(self, name, fn):
        deps = self._fence_deps()
        i = len(self.ops)
        self.ops.append(dict(engine="vector", fn=fn, deps=deps, dma=False,
                             tag=f"epoch_{name}", carrier=False,
                             inc_count=None, slot=None, slot_use=None))
        key = f"__epoch_{name}"
        self.writer[key] = i
        self.readers[key] = []
        self.epoch_key = key

    def barrier_all(self, engine="sync"):
        self.ops.append(dict(engine=engine, fn=lambda eng: eng.nop(),
                             deps=self._fence_deps(), dma=False, tag="final",
                             carrier=False, inc_count=None, slot=None,
                             slot_use=None))

    def _schedule(self):
        for op in self.ops:
            for d in op["deps"]:
                self.ops[d]["carrier"] = True
        cum = {e: 0 for e in ENGINES}
        for op in self.ops:
            if not op["dma"] and op["carrier"]:
                cum[op["engine"]] += 1
                op["inc_count"] = cum[op["engine"]]
        dims = list(ENGINES) + [(q, s) for q in DMA_QUEUES
                                for s in range(N_SLOTS[q])]
        know = {e: {d: 0 for d in dims} for e in ENGINES}
        op_know = [None] * len(self.ops)

        def dep_id(dop):
            if dop["dma"]:
                return dop["slot"], dop["slot_use"]
            return dop["engine"], dop["inc_count"]

        for i, op in enumerate(self.ops):
            e = op["engine"]
            ke = know[e]
            need = {}
            for d in op["deps"]:
                dim, c = dep_id(self.ops[d])
                need[dim] = max(need.get(dim, 0), c)
            if op["dma"] and op["slot_use"] > 1:
                s = op["slot"]
                need[s] = max(need.get(s, 0), op["slot_use"] - 1)
            waits = {dim: c for dim, c in need.items() if ke[dim] < c}
            op["waits"] = waits
            for dim, c in waits.items():
                ke[dim] = max(ke[dim], c)
            for d in op["deps"]:
                dk = op_know[d]
                for dim in dims:
                    if dk[dim] > ke[dim]:
                        ke[dim] = dk[dim]
            my = dict(ke)
            if op["dma"]:
                my[op["slot"]] = max(my[op["slot"]], op["slot_use"])
            elif op["carrier"]:
                my[e] = max(my[e], op["inc_count"])
                if e == "tensor":
                    ke[e] = my[e]
            op_know[i] = my

    def emit(self, nc, st):
        self._schedule()
        sems = {}
        for e in ENGINES:
            sems[e] = st.enter_context(nc.semaphore(f"sem_{e}"))
        for q in DMA_QUEUES:
            for s in range(N_SLOTS[q]):
                sems[(q, s)] = st.enter_context(nc.semaphore(f"ds_{q}_{s}"))
        per_engine = {e: [] for e in ENGINES}
        for op in self.ops:
            per_engine[op["engine"]].append(op)
        block = st.enter_context(nc.Block())

        def make_body(ops):
            def body(eng):
                for op in ops:
                    for dim, c in sorted(op["waits"].items(), key=str):
                        mult = 16 if isinstance(dim, tuple) else 1
                        eng.wait_ge(sems[dim], c * mult)
                    inst = op["fn"](eng)
                    if op["dma"]:
                        inst.then_inc(sems[op["slot"]], 16)
                    elif op["carrier"]:
                        inst.then_inc(sems[op["engine"]], 1)
            return body

        for e in ENGINES:
            if per_engine[e]:
                getattr(block, e)(make_body(per_engine[e]))


def build_nc():
    nc = bass.Bass("TRN2", target_bir_lowering=False)
    p = Prog()
    A = p.add
    dram = {}

    def din(name, shape, dt=F32):
        dram[name] = nc.dram_tensor(name, list(shape), dt, kind="ExternalInput")

    din("x_full", (S, D)); din("x_own", (T, D))
    din("sel0", (128, 1)); din("sel1", (128, 1))
    din("ident", (128, 128), BF16); din("ones_row", (1, 128), BF16)
    for nm in ("lng_c", "lnb_c", "ln1g_c", "ln1b_c", "ln2g_c", "ln2b_c",
               "bq_c", "bk_c"):
        din(nm, (128, 8))
    din("b1_c", (128, 32))
    din("rb1_c", (128, 128)); din("rb2_c", (128, 128))
    din("sb1_c", (128, 32)); din("sb2_c", (128, 32))
    din("lnfg_rep", (128, 1024)); din("lnfb_rep", (128, 1024))
    din("bgate_row", (1, 1024)); din("bv_row", (1, 1024)); din("bo_row", (1, 1024))
    din("b2_row", (1, 1024)); din("bf_row", (1, 1024))
    din("rb3", (NEXP, 1024)); din("sb3", (2, 1024)); din("gateb_row", (1, 8))
    din("A_f", (16, 16)); din("bdelta_c", (16, 1)); din("bin_c", (16, 1))
    din("wdelta_r", (128, 128)); din("win_r", (128, 128)); din("gatew_r", (128, 64))
    din("C_w", (16, 1024))
    din("Wq", (D, D)); din("Wk", (D, D)); din("Wv", (D, D)); din("Wo", (D, D))
    din("Wgate", (D, D)); din("W1", (D, DFF)); din("W2", (DFF, D))
    din("Wf", (2 * D, D))
    din("rw1", (NEXP, D, HE)); din("rw2", (NEXP, D, HE)); din("rw3", (NEXP, HE, D))
    din("sw1", (2, D, HE)); din("sw2", (2, D, HE)); din("sw3", (2, HE, D))

    out_h = nc.dram_tensor("out", [T, D], F32, kind="ExternalOutput")
    ytT_d = nc.dram_tensor("ytT_d", [D, S], BF16, kind="Internal")
    ctxT_d = nc.dram_tensor("ctxT_d", [D, T], BF16, kind="Internal")
    x2_d = nc.dram_tensor("x2_d", [T, D], F32, kind="Internal")
    out1_d = nc.dram_tensor("out1_d", [T, D], F32, kind="Internal")
    moT_d = nc.dram_tensor("moT_d", [2 * D, T], BF16, kind="Internal")
    y2T_d = nc.dram_tensor("y2T_d", [D, T], BF16, kind="Internal")
    gate_d = nc.dram_tensor("gate_d", [T, D], BF16, kind="Internal")

    taps = {}
    tap_specs = dict(
        ytT=([D, S], BF16), dltT=([16, S], BF16), btT=([16, S], F32),
        hall=([16, S], BF16), ctxT=([D, T], BF16), x2=([T, D], F32),
        moT=([2 * D, T], BF16), out1=([T, D], F32), negm=([128, 8], F32),
        wts=([128, 128], F32), ffn=([128, 8 * 1024], F32),
        y2T=([D, T], BF16))
    for nm in TAPS:
        shp, dt = tap_specs[nm]
        taps[nm] = nc.dram_tensor("tap_" + nm, shp, dt, kind="ExternalOutput")

    st0 = ExitStack()
    with st0:
        def sb(name, shape, dt, side=None):
            return st0.enter_context(
                nc.sbuf_tensor(name, list(shape), dt, side=side))

        psA = st0.enter_context(nc.psum_tensor("psA", [128, 1024], F32))
        psB = st0.enter_context(nc.psum_tensor("psB", [128, 1024], F32))
        ps5a = st0.enter_context(nc.psum_tensor("ps5a", [128, 512], F32))
        ps5b = st0.enter_context(nc.psum_tensor("ps5b", [128, 512], F32))
        psT = st0.enter_context(nc.psum_tensor("psT", [128, 512], F32))
        psS = st0.enter_context(nc.psum_tensor("psS", [16, 1], F32))
        PA0, PA1, PB0, PB1 = "pbA0", "pbA1", "pbB0", "pbB1"
        P5A, P5B = "pb5a", "pb5b"
        PT, PSC = "pbT", "pbSc"

        ident = sb("identc", (128, 128), BF16)
        ones_row = sb("onesr", (1, 128), BF16)
        ones_col = sb("onesc", (128, 1), BF16)
        sel0 = sb("sel0c", (128, 1), F32)
        sel1 = sb("sel1c", (128, 1), F32)
        lnp = {nm: sb("c" + nm, (128, 8), F32)
               for nm in ("lng", "lnb", "ln1g", "ln1b", "ln2g", "ln2b",
                          "bq", "bk")}
        bqs = sb("bqs", (128, 8), F32)
        b1c = sb("b1cs", (128, 32), F32)
        rb1c = sb("rb1cs", (128, 128), F32)
        rb2c = sb("rb2cs", (128, 128), F32)
        sb1c = sb("sb1cs", (128, 32), F32)
        sb2c = sb("sb2cs", (128, 32), F32)
        NROW = 1024 * 5 + 8
        rows_b = sb("rows_b", (1, NROW), BF16)
        ROFF = dict(bgate=0, bv=1024, bo=2048, b2=3072, bf=4096,
                    gateb=5120)
        A_f = sb("A_fs", (16, 16), F32)
        bdelta = sb("bdel", (16, 1), F32)
        hbdel = sb("hbdel", (16, 1), F32)
        binc = sb("bincs", (16, 1), F32)
        wdelta = sb("wdels", (128, 128), BF16)
        win = sb("wins", (128, 128), BF16)
        gatew = sb("gatews", (128, 64), BF16)
        C_bf = sb("C_bfs", (16, 1024), BF16)
        negm = sb("negms", (128, 8), F32)
        wts = sb("wtss", (128, 16 * 8), F32)
        smr = sb("smrs", (1, 64), F32)
        smi = sb("smis", (1, 8), F32)
        smb = sb("smbs", (1, 1), BF16)
        epb = sb("epbs", (1, 4), F32)
        xb = [sb(f"xb{i}", (128, 1024), F32) for i in range(2)]
        yb = [sb(f"yb{i}", (128, 1024), F32) for i in range(2)]
        dump = [sb(f"dmpb{i}", (128, 1024), BF16) for i in range(2)]
        ybf = [sb(f"ybf{i}", (128, 1024), BF16) for i in range(2)]
        st1 = [sb(f"sta{i}", (128, 1), F32) for i in range(4)]
        sw1 = sb("sw1s", (128, 8), F32)
        sw2 = sb("sw2s", (128, 8), F32)
        sw3 = sb("sw3s", (128, 8), F32)
        swi = sb("swis", (128, 8), I32)
        dump2 = sb("dmp2b", (128, 1024), BF16)
        st2 = [sb(f"stc{i}", (128, 1), F32) for i in range(4)]
        st3 = [sb(f"std{i}", (128, 1), F32) for i in range(4)]
        sti = [sb(f"sti{i}", (128, 1), I32) for i in range(4)]
        tstage = sb("tstage", (128, 1024), BF16)

        stRs = ExitStack()
        sbrs = lambda n, sh, dt: stRs.enter_context(
            nc.sbuf_tensor(n, list(sh), dt, side="right"))
        dltT = sbrs("dltTs", (16, S), BF16)
        btT = sbrs("btTs", (16, S), F32)
        hall = sbrs("halls", (16, S), BF16)
        hown = sbrs("howns", (16, T), BF16)
        asc = [sbrs(f"ascs{i}", (16, 256 * 16), BF16) for i in range(2)]
        dtmp = sbrs("dtmpS", (16, 512), F32)
        hz = sbrs("hzS", (16, 1), BF16)
        stR = ExitStack()
        sbr = lambda n, sh, dt: stR.enter_context(
            nc.sbuf_tensor(n, list(sh), dt, side="right"))
        ytg = [sbr(f"ytg{i}", (128, 4096), BF16) for i in range(2)]
        ytg2 = sbr("ytg2b", (128, 4096), BF16)
        xnb = [sbr(f"xnbb{i}", (128, 4096), BF16) for i in range(2)]
        yos = sbr("yosb", (128, 4096), BF16)

        def dma(queue, dst_ap, src_ap, reads, writes, tag=""):
            A(queue, lambda e, d=dst_ap, s=src_ap: e.dma_start(out=d, in_=s),
              reads=list(reads), writes=list(writes), dma=True, tag=tag)

        trb_state = [0]

        def transpose8(srcbuf, srckey, dst_round_view, dstkey,
                       extra_reads=(), banks=None, evac="vector"):
            if banks is None:
                banks = [(psT, PT, 0)]
            dks = dstkey if isinstance(dstkey, list) else [dstkey, dstkey]
            for r in range(2):
                ps_, bkk, off = banks[trb_state[0] % len(banks)]
                trb_state[0] += 1
                for q in range(4):
                    c = r * 4 + q
                    A("tensor",
                      lambda e, c=c, q=q, srcbuf=srcbuf, ps_=ps_, off=off:
                      e.matmul(
                          ps_[:, off + q * 128:off + (q + 1) * 128],
                          srcbuf[:, c * 128:(c + 1) * 128], ident[:],
                          start=True, stop=True),
                      reads=[srckey, "ident", bkk], writes=[bkk], tag="tr8")
                if evac == "vector":
                    A("vector", lambda e, r=r, ps_=ps_, off=off:
                      e.tensor_copy(
                        dst_round_view(r),
                        ps_[:, off:off + 512].rearrange("p (c t) -> p c t",
                                                        c=4)),
                      reads=[bkk] + list(extra_reads),
                      writes=[dks[r], bkk], tag="tr8ev")
                else:
                    A("scalar", lambda e, r=r, ps_=ps_, off=off:
                      e.activation(
                        dst_round_view(r),
                        ps_[:, off:off + 512].rearrange("p (c t) -> p c t",
                                                        c=4), AF.Identity),
                      reads=[bkk] + list(extra_reads),
                      writes=[dks[r], bkk], tag="tr8ev")

        TRB5 = [(psT, PT, 0), (psA, PA0, 0), (psA, PA1, 512),
                (psB, PB0, 0), (psB, PB1, 512)]
        TRB3 = [(psT, PT, 0), (ps5a, P5A, 0), (ps5b, P5B, 0)]

        def tap_copy(name, src_ap, rkeys):
            if name in taps:
                dma("sync", taps[name].ap(), src_ap, rkeys, [f"tap_{name}"],
                    tag=f"tap{name}")

        def rsqrt_small(vap, vkey, yap, iap, scratch_ap, wkeys):
            A("vector", lambda e: e.tensor_scalar(
                iap, vap.bitcast(I32), 1, None, ALU.logical_shift_right),
              reads=[vkey] + wkeys, writes=wkeys)
            A("vector", lambda e: e.tensor_scalar(
                iap, iap, -1, int(MAGIC), ALU.mult, ALU.add),
              reads=wkeys, writes=wkeys)
            A("vector", lambda e: e.tensor_copy(yap, iap.bitcast(F32)),
              reads=wkeys, writes=wkeys)
            for _ in range(2):
                A("vector", lambda e: e.tensor_mul(scratch_ap, yap, yap),
                  reads=wkeys, writes=wkeys)
                A("vector", lambda e: e.tensor_mul(scratch_ap, scratch_ap,
                                                   vap),
                  reads=wkeys + [vkey], writes=wkeys)
                A("vector", lambda e: e.tensor_scalar(
                    scratch_ap, scratch_ap, -0.5, 1.5, ALU.mult, ALU.add),
                  reads=wkeys, writes=wkeys)
                A("vector", lambda e: e.tensor_mul(yap, yap, scratch_ap),
                  reads=wkeys, writes=wkeys)

        def ln_tile(xkey, xap, slot, dmpsel, n=1024):
            s_sum, s_ss, s_rs, s_i = st1[slot], st2[slot], st3[slot], sti[slot]
            dmp = dump[dmpsel]
            kw = f"LNS{slot}"
            dk = f"dmp{dmpsel}"
            A("vector", lambda e: e.scalar_tensor_tensor(
                dmp[:, 0:n], xap, 1.0, xap, ALU.mult, ALU.mult,
                accum_out=s_ss[:, 0:1]),
              reads=[xkey, kw], writes=[kw, dk], tag="ln_sq")
            A("vector", lambda e: e.tensor_scalar(
                dmp[:, 0:n], xap, 1.0, 0.0, ALU.mult, ALU.add,
                accum_out=s_sum[:, 0:1]),
              reads=[xkey, kw], writes=[kw, dk], tag="ln_sum")
            A("vector", lambda e: e.tensor_scalar_mul(
                s_sum[:, 0:1], s_sum[:, 0:1], 1.0 / n),
              reads=[kw], writes=[kw])
            A("vector", lambda e: e.tensor_mul(s_rs[:, 0:1], s_sum[:, 0:1],
                                               s_sum[:, 0:1]),
              reads=[kw], writes=[kw])
            A("vector", lambda e: e.scalar_tensor_tensor(
                s_ss[:, 0:1], s_ss[:, 0:1], 1.0 / n, s_rs[:, 0:1],
                ALU.mult, ALU.subtract),
              reads=[kw], writes=[kw])
            A("vector", lambda e: e.tensor_scalar_add(s_ss[:, 0:1],
                                                      s_ss[:, 0:1], 1e-5),
              reads=[kw], writes=[kw])
            rsqrt_small(s_ss[:, 0:1], kw, s_rs[:, 0:1], s_i[:, 0:1],
                        s_i[:, 0:1].bitcast(F32), [kw])
            A("vector", lambda e: e.scalar_tensor_tensor(
                s_sum[:, 0:1], s_rs[:, 0:1], -1.0, s_sum[:, 0:1],
                ALU.mult, ALU.mult),
              reads=[kw], writes=[kw], tag="nmr")
            return s_sum, s_rs, kw

        def ln_group(xkeys, xaps, dmpsel, n=1024):
            k = len(xaps)
            co = dmpsel * 4
            K1, K2, K3 = f"LW1{dmpsel}", f"LW2{dmpsel}", f"LW3{dmpsel}"
            dk = f"dmp{dmpsel}"
            dmp = dump[dmpsel]
            for j, (xk, xap) in enumerate(zip(xkeys, xaps)):
                A("scalar", lambda e, xap=xap, j=j: e.activation(
                    dmp[:, 0:n], xap, AF.Square,
                    accum_out=sw2[:, co + j:co + j + 1]),
                  reads=[xk], writes=[K2, dk], tag="lng_sq")
                A("vector", lambda e, xap=xap, j=j: e.tensor_scalar(
                    dump2[:, 0:n], xap, 1.0, 0.0, ALU.mult, ALU.add,
                    accum_out=sw1[:, co + j:co + j + 1]),
                  reads=[xk], writes=[K1, "dmp2"], tag="lng_sum")
            A("vector", lambda e: e.tensor_scalar_mul(sw1[:, co:co + k],
                                                      sw1[:, co:co + k],
                                                      1.0 / n),
              reads=[K1], writes=[K1])
            A("vector", lambda e: e.tensor_mul(sw3[:, co:co + k],
                                               sw1[:, co:co + k],
                                               sw1[:, co:co + k]),
              reads=[K1], writes=[K3])
            A("vector", lambda e: e.scalar_tensor_tensor(
                sw2[:, co:co + k], sw2[:, co:co + k], 1.0 / n,
                sw3[:, co:co + k], ALU.mult, ALU.subtract),
              reads=[K2, K3], writes=[K2])
            A("vector", lambda e: e.tensor_scalar_add(sw2[:, co:co + k],
                                                      sw2[:, co:co + k],
                                                      1e-5),
              reads=[K2], writes=[K2])
            rsqrt_small(sw2[:, co:co + k], K2, sw3[:, co:co + k],
                        swi[:, co:co + k], swi[:, co:co + k].bitcast(F32),
                        [K3])
            A("vector", lambda e: e.scalar_tensor_tensor(
                sw1[:, co:co + k], sw3[:, co:co + k], -1.0,
                sw1[:, co:co + k], ALU.mult, ALU.mult),
              reads=[K1, K3], writes=[K1], tag="nmrw")
            return sw1, sw3, [K1, K3], co

        def row_bf(key, off, ln_):
            return rows_b[:, ROFF[key] + off: ROFF[key] + off + ln_]

        scan_state = dict(t=0, built=0, dgroups=0, tctr=0, inpump=False,
                          credit=0.0, target=600.0)
        TOP_COST = dict(smm=216, cmm=85, kmm=216, vmm=115, qmm=216, dmm=220,
                        bmm=220, womm=216, wob=216, gmm=216, gb=216, hC=216,
                        tr8=85, x2tr=85, ctr=85, w1mm=216, w2mm=216, b2b=216,
                        wfmm=216, bfb=216, p1mm=216, p2mm=216, w3mm=216,
                        b3b=216, mbc=60, nmm=216, gsc=60, gscb=60, vb=115)

        _A_raw = p.add

        VS_COST = dict(lng_sq=900, lng_sum=900, yhat=900, x2h=900, fnorm=900,
                       y2h=900, fg=900, fbias=900, o1=900, gsel0=1500,
                       gsel1=1500, p4y=900, bmy=900, fev=500, tr8ev=500,
                       xn2ev=500, exp=800, kev=400, vev=300, qev=400,
                       xnT=500, xn1T=500, qxn=500, gxn=500, relu=400,
                       momul=500, moadd=900, mocast=500, ln_sq=900,
                       ln_sum=900)

        def A(engine, fn, reads=(), writes=(), dma=False, tag=""):
            idx = _A_raw(engine, fn, reads=reads, writes=writes, dma=dma,
                         tag=tag)
            if scan_state["inpump"] or tag in ("scmm", "sctanh", "ascb"):
                return idx
            if engine == "tensor":
                scan_state["credit"] += TOP_COST.get(tag, 150)
            elif (engine in ("vector", "scalar")
                  and scan_state["target"] <= 600.0):
                scan_state["credit"] += VS_COST.get(tag, 260)
            else:
                return idx
            if scan_state["credit"] >= scan_state["target"]:
                scan_state["inpump"] = True
                n = min(int(scan_state["credit"] // scan_state["target"]), 2)
                scan_state["credit"] -= n * scan_state["target"]
                pump_scan(n)
                scan_state["inpump"] = False
            return idx

        def build_asc_chunk(k):
            ring = asc[k % 2]
            rk = f"ascr{k % 2}"
            gsl = slice(k * 256, (k + 1) * 256)
            for j in range(16):
                A("vector", lambda e, j=j, ring=ring, gsl=gsl:
                  e.tensor_scalar_mul(
                      ring[:].rearrange("p (t j) -> p t j", j=16)[:, :, j],
                      dltT[:, gsl], A_f[:, j:j + 1]),
                  reads=[f"dltT{k // 2}", "A_f"], writes=[rk], tag="ascb")

        def pump_scan(n):
            for _ in range(n):
                t = scan_state["t"]
                if t >= S:
                    return
                k = t // 256
                if k >= scan_state["built"]:
                    if k // 2 >= scan_state["dgroups"]:
                        return
                    build_asc_chunk(k)
                    scan_state["built"] = k + 1
                ring = asc[k % 2]
                rk = f"ascr{k % 2}"
                hprev = hz[:, 0:1] if t == 0 else hall[:, t - 1:t]
                hprevk = "hz" if t == 0 else f"h{t - 1}"
                off = (t % 256) * 16
                A("tensor", lambda e, ring=ring, off=off, hp=hprev: e.matmul(
                    psS[0:16, 0:1], ring[:, off:off + 16], hp,
                    start=True, stop=True),
                  reads=[rk, hprevk, PSC], writes=[PSC], tag="scmm")
                A("scalar", lambda e, t=t: e.activation(
                    hall[:, t:t + 1], psS[0:16, 0:1], AF.Tanh,
                    bias=btT[:, t:t + 1]),
                  reads=[PSC, f"btT{t // 512}"], writes=[f"h{t}", PSC],
                  tag="sctanh")
                scan_state["t"] = t + 1

        dma("sync", ident[:], dram["ident"].ap(), [], ["ident"])
        dma("sync", ones_row[:], dram["ones_row"].ap(), [], ["ones_row"])
        dma("sync", sel0[:], dram["sel0"].ap(), [], ["sel0"])
        dma("sync", sel1[:], dram["sel1"].ap(), [], ["sel1"])
        for nm, t_ in (("lng", "lng_c"), ("lnb", "lnb_c"), ("ln1g", "ln1g_c"),
                       ("ln1b", "ln1b_c"), ("ln2g", "ln2g_c"),
                       ("ln2b", "ln2b_c"), ("bq", "bq_c"), ("bk", "bk_c")):
            dma("sync", lnp[nm][:], dram[t_].ap(), [], [nm])
        dma("sync", b1c[:], dram["b1_c"].ap(), [], ["b1c"])
        dma("sync", rb1c[:], dram["rb1_c"].ap(), [], ["rb1c"])
        dma("sync", rb2c[:], dram["rb2_c"].ap(), [], ["rb2c"])
        dma("sync", sb1c[:], dram["sb1_c"].ap(), [], ["sb1c"])
        dma("sync", sb2c[:], dram["sb2_c"].ap(), [], ["sb2c"])
        for nm, key in (("bgate_row", "bgate"), ("bv_row", "bv"),
                        ("bo_row", "bo"), ("b2_row", "b2"), ("bf_row", "bf"),
                        ("gateb_row", "gateb")):
            ln_ = dram[nm].shape[1]
            dma("gpsimd", rows_b[:, ROFF[key]:ROFF[key] + ln_],
                dram[nm].ap(), ["rows_b"], ["rows_b"])
        dma("sync", A_f[:], dram["A_f"].ap(), [], ["A_f"])
        dma("sync", bdelta[:], dram["bdelta_c"].ap(), [], ["bdeltak"])
        dma("sync", binc[:], dram["bin_c"].ap(), [], ["binck"])
        dma("gpsimd", wdelta[:], dram["wdelta_r"].ap(), [], ["wdelta"])
        dma("gpsimd", win[:], dram["win_r"].ap(), [], ["win"])
        dma("gpsimd", gatew[:], dram["gatew_r"].ap(), [], ["gatew"])
        dma("gpsimd", C_bf[:], dram["C_w"].ap(), [], ["C_bf"])
        A("vector", lambda e: e.memset(ones_col[:], 1.0), writes=["ones_col"])
        A("vector", lambda e: e.memset(hz[:], 0.0), writes=["hz"])
        A("vector", lambda e: e.tensor_scalar_mul(bqs[:], lnp["bq"][:], RSQ),
          reads=["bq"], writes=["bqs"])
        A("vector", lambda e: e.tensor_scalar_mul(hbdel[:], bdelta[:], 0.5),
          reads=["bdeltak"], writes=["hbdel"])

        st23 = ExitStack()
        sbq = lambda n, sh, dt: st23.enter_context(
            nc.sbuf_tensor(n, list(sh), dt))
        kT = sbq("kTs", (128, 2 * S), BF16)
        vsb = sbq("vsbs", (128, 32 * 258), BF16)
        wkh = sbq("wkhs", (128, 8 * 256), BF16)
        wvh = sbq("wvhs", (128, 8 * 256), BF16)
        stP1 = ExitStack()
        xbig = stP1.enter_context(
            nc.sbuf_tensor("xbigs", [128, 4096], F32, side="right"))

        def emit_kv_group(hg, g, gb):
            xnk = f"xnb{gb}"
            for c in range(8):
                A("scalar", lambda e, b_=gb, c=c: e.activation(
                    xnb[b_][:, c * 512:(c + 1) * 512],
                    ytg[b_][:, c * 512:(c + 1) * 512], AF.Identity,
                    bias=lnp["ln1b"][:, c:c + 1],
                    scale=lnp["ln1g"][:, c:c + 1]),
                  reads=[f"ytg{gb}_h{c // 4}", "ln1g", "ln1b",
                         xnk + f"_{c}"],
                  writes=[xnk + f"_{c}"], tag="xn1T")
            for fc in range(2):
                pse, pk = (ps5a, P5A) if fc == 0 else (ps5b, P5B)
                for c in range(8):
                    A("tensor", lambda e, b_=gb, c=c, fc=fc, pse=pse:
                      e.matmul(pse[:, 0:512],
                               wkh[:, c * 256 + fc * 128:
                                   c * 256 + (fc + 1) * 128],
                               xnb[b_][:, c * 512:(c + 1) * 512],
                               start=(c == 0), stop=(c == 7)),
                      reads=[xnk + f"_{c}", "wkh", pk], writes=[pk],
                      tag="kmm")
                A("vector", lambda e, fc=fc, g=g, pse=pse, hg=hg:
                  e.tensor_scalar(kT[:, fc * S + g * 512:
                                     fc * S + (g + 1) * 512],
                                  pse[:, 0:512],
                                  lnp["bk"][:, hg * 2 + fc:
                                            hg * 2 + fc + 1], None,
                                  ALU.add),
                  reads=[pk, "bk", f"kT{fc}_{g}"],
                  writes=[f"kT{fc}_{g}", pk], tag="kev")
            for j2 in range(4):
                i2 = g * 4 + j2
                pse, pk = (ps5a, P5A) if j2 % 2 == 0 else (ps5b, P5B)
                for c in range(8):
                    A("tensor", lambda e, b_=gb, c=c, j2=j2, pse=pse:
                      e.matmul(pse[:, 0:256],
                               xnb[b_][:, c * 512 + j2 * 128:
                                   c * 512 + (j2 + 1) * 128],
                               wvh[:, c * 256:(c + 1) * 256],
                               start=(c == 0), stop=False),
                      reads=[xnk + f"_{c}", "wvh", pk], writes=[pk],
                      tag="vmm")
                A("tensor", lambda e, pse=pse, hg=hg: e.matmul(
                    pse[:, 0:256], ones_row[:],
                    row_bf("bv", hg * 256, 256), start=False, stop=True),
                  reads=["ones_row", "rows_b", pk], writes=[pk], tag="vb")
                A("vector", lambda e, i2=i2, pse=pse: e.tensor_copy(
                    vsb[:, i2 * 258:(i2 + 1) * 258].rearrange(
                        "p (h c) -> p h c", h=2)[:, :, 0:128],
                    pse[:, 0:256].rearrange("p (h c) -> p h c", h=2)),
                  reads=[pk, f"vsb{i2}"], writes=[f"vsb{i2}", pk],
                  tag="vev")

        dma("gpsimd", wkh[:].rearrange("p (c f) -> p c f", c=8),
            dram["Wk"].ap()[:, 0:256].rearrange("(c p) f -> p c f", p=128),
            ["wkh"], ["wkh"], tag="wk")
        dma("gpsimd", wvh[:].rearrange("p (c f) -> p c f", c=8),
            dram["Wv"].ap()[:, 0:256].rearrange("(c p) f -> p c f", p=128),
            ["wvh"], ["wvh"], tag="wv")
        A("vector", lambda e: e.memset(
            vsb[:].rearrange("p (i h c) -> p i h c", i=32, h=2)[
                :, :, :, 128:129], 1.0),
          reads=[f"vsb{i}" for i in range(32)],
          writes=[f"vsb{i}" for i in range(32)], tag="vones")
        lng_ret = {}

        def p1_load_ln(g):
            for j in range(4):
                i = g * 4 + j
                dma("sync", xbig[:, j * 1024:(j + 1) * 1024],
                    dram["x_full"].ap()[i * 128:(i + 1) * 128, :],
                    [f"xbg{j}"], [f"xbg{j}"], tag=f"p1x{i}")
            lng_ret[g] = ln_group(
                [f"xbg{j}" for j in range(4)],
                [xbig[:, j * 1024:(j + 1) * 1024]
                 for j in range(4)], g % 2)

        p1_load_ln(0)
        for g in range(8):
            gb = g % 2
            nmrw, rstdw, lks, lco = lng_ret[g]
            for j in range(4):
                i = g * 4 + j
                bsel = i % 2
                ybk = f"ybf{bsel}"
                A("vector", lambda e, b_=bsel, j=j, lco=lco:
                  e.tensor_scalar(
                    ybf[b_][:], xbig[:, j * 1024:(j + 1) * 1024],
                    rstdw[:, lco + j:lco + j + 1],
                    nmrw[:, lco + j:lco + j + 1],
                    ALU.mult, ALU.add),
                  reads=[f"xbg{j}"] + lks, writes=[ybk], tag="yhat")
                transpose8(ybf[bsel][:], ybk,
                           lambda r, b_=gb, j=j: ytg[b_][:].rearrange(
                               "p (c t) -> p c t", c=8)[
                               :, r * 4:(r + 1) * 4, j * 128:(j + 1) * 128],
                           [f"ytg{gb}_h0", f"ytg{gb}_h1"], banks=TRB5)
            dma("sync",
                ytT_d.ap().rearrange("(c p) s -> p c s", p=128)[
                    :, :, g * 512:(g + 1) * 512],
                ytg[gb][:].rearrange("p (c t) -> p c t", c=8),
                [f"ytg{gb}_h0", f"ytg{gb}_h1"], [f"ytTd{g}"], tag="yspill")
            xnk = f"xnb{gb}"
            for c in range(8):
                A("scalar", lambda e, b_=gb, c=c: e.activation(
                    xnb[b_][:, c * 512:(c + 1) * 512],
                    ytg[b_][:, c * 512:(c + 1) * 512], AF.Identity,
                    bias=lnp["lnb"][:, c:c + 1], scale=lnp["lng"][:, c:c + 1]),
                  reads=[f"ytg{gb}_h{c // 4}", "lng", "lnb", xnk + f"_{c}"],
                  writes=[xnk + f"_{c}"], tag="xnT")
            if g < 7:
                p1_load_ln(g + 1)
            for c in range(8):
                A("tensor", lambda e, b_=gb, c=c: e.matmul(
                    ps5a[0:16, 0:512], wdelta[:, c * 16:(c + 1) * 16],
                    xnb[b_][:, c * 512:(c + 1) * 512],
                    start=(c == 0), stop=(c == 7)),
                  reads=[xnk + f"_{c}", "wdelta", P5A], writes=[P5A],
                  tag="dmm")
            for c in range(8):
                A("tensor", lambda e, b_=gb, c=c: e.matmul(
                    ps5b[0:16, 0:512], win[:, c * 16:(c + 1) * 16],
                    xnb[b_][:, c * 512:(c + 1) * 512],
                    start=(c == 0), stop=(c == 7)),
                  reads=[xnk + f"_{c}", "win", P5B], writes=[P5B], tag="bmm")
            A("scalar", lambda e: e.activation(
                dtmp[:, 0:512], ps5a[0:16, 0:512], AF.Tanh,
                bias=hbdel[:, 0:1], scale=0.5),
              reads=[P5A, "hbdel", "dtmpk"], writes=["dtmpk", P5A],
              tag="dsig")
            A("vector", lambda e, g=g: e.tensor_scalar(
                dltT[:, g * 512:(g + 1) * 512], dtmp[:, 0:512], 0.5, 0.5,
                ALU.mult, ALU.add),
              reads=["dtmpk"], writes=[f"dltT{g}"], tag="dfin")
            A("vector", lambda e, g=g: e.tensor_scalar(
                btT[:, g * 512:(g + 1) * 512], ps5b[0:16, 0:512],
                binc[:, 0:1], None, ALU.add),
              reads=[P5B, "binck"], writes=[f"btT{g}", P5B], tag="btf")
            scan_state["dgroups"] = g + 1
            pump_scan(8)
            emit_kv_group(0, g, gb)

        tap_copy("ytT", ytT_d.ap(), [f"ytTd{g}" for g in range(8)])
        if "dltT" in taps:
            dma("sync", taps["dltT"].ap(), dltT[:],
                [f"dltT{g}" for g in range(8)], ["tap_dltT"])
        if "btT" in taps:
            dma("sync", taps["btT"].ap(), btT[:],
                [f"btT{g}" for g in range(8)], ["tap_btT"])

        scan_state["target"] = 600.0
        stP1.close()
        p.epoch("p23", lambda e: e.memset(epb[0:1, 0:1], 0.0))
        st23b = ExitStack()
        sbq2 = lambda n, sh, dt: st23b.enter_context(
            nc.sbuf_tensor(n, list(sh), dt))
        qT = sbq2("qTs", (128, 2 * T), BF16)
        expb = [sbq2(f"expbs{i}", (128, 1024), BF16) for i in range(3)]
        ctxe = sbq2("ctxes", (128, 512), BF16)
        ctxb = sbq2("ctxbs", (128, 512), BF16)

        for hg in range(4):
            if hg > 0:
                dma("gpsimd", wkh[:].rearrange("p (c f) -> p c f", c=8),
                    dram["Wk"].ap()[:, hg * 256:(hg + 1) * 256].rearrange(
                        "(c p) f -> p c f", p=128),
                    ["wkh"], ["wkh"], tag="wk")
                dma("gpsimd", wvh[:].rearrange("p (c f) -> p c f", c=8),
                    dram["Wv"].ap()[:, hg * 256:(hg + 1) * 256].rearrange(
                        "(c p) f -> p c f", p=128),
                    ["wvh"], ["wvh"], tag="wv")
                A("vector", lambda e: e.memset(
                    vsb[:].rearrange("p (i h c) -> p i h c", i=32, h=2)[
                        :, :, :, 128:129], 1.0),
                  reads=[f"vsb{i}" for i in range(32)],
                  writes=[f"vsb{i}" for i in range(32)], tag="vones")
                scan_state["target"] = 600.0
                for g in range(8):
                    gb = g % 2
                    for hf in range(2):
                        dma("sync", ytg[gb][:, hf * 2048:(hf + 1) * 2048],
                            ytT_d.ap().rearrange("(c p) s -> p c s", p=128)[
                                :, hf * 4:(hf + 1) * 4,
                                g * 512:(g + 1) * 512],
                            [f"ytg{gb}_h{hf}"], [f"ytg{gb}_h{hf}"],
                            tag="yld")
                    emit_kv_group(hg, g, gb)

            dma("gpsimd", wkh[:].rearrange("p (c f) -> p c f", c=8),
                dram["Wq"].ap()[:, hg * 256:(hg + 1) * 256].rearrange(
                    "(c p) f -> p c f", p=128),
                ["wkh"], ["wkh"], tag="wq")
            for go in range(4):
                dma("sync", ytg[0][:],
                    ytT_d.ap().rearrange("(c p) s -> p c s", p=128)[
                        :, :, go * 512:(go + 1) * 512],
                    [f"ytTd{go}", "ytg0_h0", "ytg0_h1"],
                    ["ytg0_h0", "ytg0_h1"], tag="qylda")
                dma("sync", ytg2[:],
                    ytT_d.ap().rearrange("(c p) s -> p c s", p=128)[
                        :, :, T + go * 512:T + (go + 1) * 512],
                    [f"ytTd{4 + go}", "ytg2"], ["ytg2"], tag="qyldb")
                A("vector", lambda e: e.tensor_scalar_mul(
                    yos[:], ytg[0][:], sel0[:, 0:1]),
                  reads=["ytg0_h0", "ytg0_h1", "sel0", "yos"],
                  writes=["yos"], tag="sel0")
                A("vector", lambda e: e.scalar_tensor_tensor(
                    yos[:], ytg2[:], sel1[:, 0:1], yos[:], ALU.mult, ALU.add),
                  reads=["ytg2", "sel1"], writes=["yos"], tag="sel1")
                for c in range(8):
                    A("scalar", lambda e, c=c: e.activation(
                        xnb[0][:, c * 512:(c + 1) * 512],
                        yos[:, c * 512:(c + 1) * 512], AF.Identity,
                        bias=lnp["ln1b"][:, c:c + 1],
                        scale=lnp["ln1g"][:, c:c + 1]),
                      reads=["yos", "ln1g", "ln1b", f"xnb0_{c}"],
                      writes=[f"xnb0_{c}"], tag="qxn")
                for fc in range(2):
                    pse, pk = (ps5a, P5A) if fc == 0 else (ps5b, P5B)
                    for c in range(8):
                        A("tensor", lambda e, c=c, fc=fc, pse=pse: e.matmul(
                            pse[:, 0:512],
                            wkh[:, c * 256 + fc * 128:c * 256 + (fc + 1) * 128],
                            xnb[0][:, c * 512:(c + 1) * 512],
                            start=(c == 0), stop=(c == 7)),
                          reads=[f"xnb0_{c}", "wkh", pk], writes=[pk],
                          tag="qmm")
                    A("vector", lambda e, fc=fc, go=go, pse=pse, hg=hg:
                      e.tensor_scalar(qT[:, fc * T + go * 512:
                                         fc * T + (go + 1) * 512],
                                      pse[:, 0:512], RSQ,
                                      bqs[:, hg * 2 + fc:hg * 2 + fc + 1],
                                      ALU.mult, ALU.add),
                      reads=[pk, "bqs", f"qT{fc}_{go}"],
                      writes=[f"qT{fc}_{go}", pk], tag="qev")

            for hl in range(2):
                h = hg * 2 + hl
                for part, nch, src_, srck in (
                        (0, 4, kT, "kT"), (2, 2, qT, "qT")):
                    for bch in range(nch):
                        A("scalar", lambda e, hl=hl, bch=bch, src_=src_,
                          part=part: e.activation(
                              expb[0][:],
                              src_[:, hl * (S if part == 0 else T) +
                                   bch * 1024:
                                   hl * (S if part == 0 else T) +
                                   (bch + 1) * 1024],
                              AF.Square),
                          reads=[f"{srck}{hl}_{2 * bch}",
                                 f"{srck}{hl}_{2 * bch + 1}", "expb0"],
                          writes=["expb0"], tag="sq")
                        for ch in range(2):
                            A("tensor", lambda e, ch=ch: e.matmul(
                                ps5a[0:1, 0:512], ones_col[:],
                                expb[0][:, ch * 512:(ch + 1) * 512],
                                start=True, stop=True),
                              reads=["expb0", "ones_col", P5A], writes=[P5A],
                              tag="nmm")
                            first = (bch == 0 and ch == 0)
                            A("vector", lambda e, part=part, first=first:
                              e.reduce_max(
                                  smr[0:1, part:part + 1] if first
                                  else smr[0:1, 1:2],
                                  ps5a[0:1, 0:512], axis=AX.X),
                              reads=[P5A, "smr"], writes=["smr", P5A],
                              tag="nmax")
                            if not first:
                                A("vector", lambda e, part=part: e.tensor_max(
                                    smr[0:1, part:part + 1],
                                    smr[0:1, part:part + 1], smr[0:1, 1:2]),
                                  reads=["smr"], writes=["smr"], tag="nacc")
                A("vector", lambda e: e.tensor_mul(
                    smr[0:1, 3:4], smr[0:1, 0:1], smr[0:1, 2:3]),
                  reads=["smr"], writes=["smr"], tag="mz")
                rsqrt_small(smr[0:1, 3:4], "smr", smr[0:1, 4:5],
                            smi[0:1, 0:1].bitcast(I32), smr[0:1, 5:6],
                            ["smr"])
                A("vector", lambda e: e.tensor_mul(
                    smr[0:1, 6:7], smr[0:1, 3:4], smr[0:1, 4:5]),
                  reads=["smr"], writes=["smr"], tag="msqrt")
                A("vector", lambda e: e.tensor_scalar(
                    smr[0:1, 6:7], smr[0:1, 6:7], 1.02, 0.01, ALU.mult,
                    ALU.add),
                  reads=["smr"], writes=["smr"], tag="mpad")
                A("vector", lambda e: e.tensor_scalar_min(
                    smr[0:1, 6:7], smr[0:1, 6:7], 30.0),
                  reads=["smr"], writes=["smr"], tag="mclip")
                A("vector", lambda e: e.tensor_copy(smb[0:1, 0:1],
                                                    smr[0:1, 6:7]),
                  reads=["smr"], writes=["smbk"], tag="mcast")
                A("tensor", lambda e: e.matmul(
                    ps5b[:, 0:1], ones_row[:], smb[0:1, 0:1],
                    start=True, stop=True),
                  reads=["smbk", "ones_row", P5B], writes=[P5B], tag="mbc")
                A("vector", lambda e, h=h: e.tensor_scalar_mul(
                    negm[:, h:h + 1], ps5b[:, 0:1], -1.0),
                  reads=[P5B, "negm"], writes=["negm", P5B], tag="mneg")

            scan_state["target"] = 1100.0
            for hl in range(2):
                h = hg * 2 + hl

                def emit_scores(pr, hl=hl, qg_ref=None):
                    pass

                for qg in range(4):
                    def scores_of(pr):
                        pse, pks = (psA, [PA0, PA1]) if pr % 2 == 0 \
                            else (psB, [PB0, PB1])
                        for u in range(2):
                            skc = pr * 2 + u
                            A("tensor",
                              lambda e, hl=hl, qg=qg, skc=skc, u=u, pse=pse:
                              e.matmul(pse[:, u * 512:(u + 1) * 512],
                                       kT[:, hl * S + skc * 128:
                                           hl * S + (skc + 1) * 128],
                                       qT[:, hl * T + qg * 512:
                                           hl * T + (qg + 1) * 512],
                                       start=True, stop=True),
                              reads=[f"kT{hl}_{skc // 4}",
                                     f"qT{hl}_{qg}", pks[u]],
                              writes=[pks[u]], tag="smm")

                    def exp_of(pr):
                        pse, pks = (psA, [PA0, PA1]) if pr % 2 == 0 \
                            else (psB, [PB0, PB1])
                        ek = pr % 3
                        A("scalar", lambda e, h=h, pse=pse, ek=ek:
                          e.activation(expb[ek][:], pse[:], AF.Exp,
                                       bias=negm[:, h:h + 1]),
                          reads=pks + ["negm", f"expb{ek}"],
                          writes=[f"expb{ek}"] + pks, tag="exp")

                    def ctx_of(pr):
                        ek = pr % 3
                        for u in range(2):
                            skc = pr * 2 + u
                            for qs in range(4):
                                cps = ps5a if qs < 2 else ps5b
                                cpk = P5A if qs < 2 else P5B
                                co = (qs % 2) * 129
                                A("tensor",
                                  lambda e, u=u, qs=qs, skc=skc, hl=hl,
                                  ek=ek, cps=cps, co=co:
                                  e.matmul(cps[:, co:co + 129],
                                           expb[ek][:, u * 512 + qs * 128:
                                                     u * 512 + (qs + 1) * 128],
                                           vsb[:, skc * 258 + hl * 129:
                                               skc * 258 + (hl + 1) * 129],
                                           start=(skc == 0 and qs % 2 == 0),
                                           stop=(skc == 31 and qs % 2 == 1),
                                           skip_group_check=True),
                                  reads=[f"expb{ek}", f"vsb{skc}", cpk],
                                  writes=[cpk], tag="cmm")

                    for pr in range(16):
                        scores_of(pr)
                        if pr > 1:
                            ctx_of(pr - 2)
                        exp_of(pr)
                    ctx_of(14)
                    ctx_of(15)
                    for qs in range(4):
                        cps = ps5a if qs < 2 else ps5b
                        cpk = P5A if qs < 2 else P5B
                        co = (qs % 2) * 129
                        A("vector", lambda e, cps=cps, co=co, qs=qs:
                          e.reciprocal(st1[qs][:, 0:1],
                                       cps[:, co + 128:co + 129]),
                          reads=[cpk, f"crec{qs}"], writes=[f"crec{qs}"],
                          tag="crec")
                        A("vector", lambda e, cps=cps, co=co, qs=qs:
                          e.tensor_scalar_mul(ctxe[:, qs * 128:(qs + 1) * 128],
                                              cps[:, co:co + 128],
                                              st1[qs][:, 0:1]),
                          reads=[cpk, f"crec{qs}", "ctxe"],
                          writes=["ctxe", cpk], tag="cev")
                        A("tensor", lambda e, qs=qs: e.matmul(
                            psT[:, qs * 128:(qs + 1) * 128],
                            ctxe[:, qs * 128:(qs + 1) * 128], ident[:],
                            start=True, stop=True),
                          reads=["ctxe", "ident", PT], writes=[PT], tag="ctr")
                    A("vector", lambda e: e.tensor_copy(ctxb[:],
                                                        psT[:, 0:512]),
                      reads=[PT, "ctxb"], writes=["ctxb", PT], tag="ctev")
                    dma("sync", ctxT_d.ap()[h * 128:(h + 1) * 128,
                                            qg * 512:(qg + 1) * 512],
                        ctxb[:], ["ctxb"], [f"ctxd{h}_{qg}"], tag="cspill")

        tap_copy("negm", negm[:], ["negm"])
        tap_copy("ctxT", ctxT_d.ap(),
                 [f"ctxd{h}_{qg}" for h in range(8) for qg in range(4)])

        if "hall" in taps:
            pump_scan(S)
            dma("sync", taps["hall"].ap(), hall[:],
                [f"h{t}" for t in range(S)], ["tap_hall"])

        st23b.close()
        st23.close()
        p.epoch("p4a", lambda e: e.memset(epb[0:1, 0:1], 0.0))

        scan_state["target"] = 600.0
        st4a = ExitStack()
        sb4a = lambda n, sh, dt: st4a.enter_context(
            nc.sbuf_tensor(n, list(sh), dt))
        wos = sb4a("woss", (128, 8 * 1024), BF16)
        wgs = sb4a("wgss", (128, 8 * 1024), BF16)
        ctb = [sb4a(f"ctbs{i}", (128, 1024), BF16) for i in range(2)]
        gtb = sb4a("gtbs", (128, 1024), BF16)
        gsc = sb4a("gscs", (128, 1024), BF16)
        zb4 = sb4a("zb4s", (128, 1024), F32)

        for hf in range(2):
            dma("gpsimd",
                wos[:, hf * 4096:(hf + 1) * 4096].rearrange(
                    "p (c f) -> p c f", c=4),
                dram["Wo"].ap()[hf * 512:(hf + 1) * 512, :].rearrange(
                    "(c p) f -> p c f", p=128),
                [], [f"wos_h{hf}"], tag="wo")
            dma("gpsimd",
                wgs[:, hf * 4096:(hf + 1) * 4096].rearrange(
                    "p (c f) -> p c f", c=4),
                dram["Wgate"].ap()[hf * 512:(hf + 1) * 512, :].rearrange(
                    "(c p) f -> p c f", p=128),
                [], [f"wgs_h{hf}"], tag="wg")

        for go in range(4):
            dma("sync", ytg[0][:],
                ytT_d.ap().rearrange("(c p) s -> p c s", p=128)[
                    :, :, go * 512:(go + 1) * 512],
                [f"ytTd{go}", "ytg0_h0", "ytg0_h1"],
                ["ytg0_h0", "ytg0_h1"], tag="gylda")
            dma("sync", ytg2[:],
                ytT_d.ap().rearrange("(c p) s -> p c s", p=128)[
                    :, :, T + go * 512:T + (go + 1) * 512],
                [f"ytTd{4 + go}", "ytg2"], ["ytg2"], tag="gyldb")
            A("vector", lambda e: e.tensor_scalar_mul(
                yos[:], ytg[0][:], sel0[:, 0:1]),
              reads=["ytg0_h0", "ytg0_h1", "sel0", "yos"],
              writes=["yos"], tag="gsel0")
            A("vector", lambda e: e.scalar_tensor_tensor(
                yos[:], ytg2[:], sel1[:, 0:1], yos[:], ALU.mult, ALU.add),
              reads=["ytg2", "sel1"], writes=["yos"], tag="gsel1")
            for c in range(8):
                A("scalar", lambda e, c=c: e.activation(
                    xnb[0][:, c * 512:(c + 1) * 512],
                    yos[:, c * 512:(c + 1) * 512], AF.Identity,
                    bias=lnp["lnb"][:, c:c + 1], scale=lnp["lng"][:, c:c + 1]),
                  reads=["yos", "lng", "lnb", f"xnb0_{c}"],
                  writes=[f"xnb0_{c}"], tag="gxn")

            for j in range(4):
                i = go * 4 + j
                bsel = i % 2
                xk = f"xb{bsel}"
                dma("sync", xb[bsel][:],
                    dram["x_own"].ap()[i * 128:(i + 1) * 128, :],
                    [xk], [xk], tag=f"p4x{i}")
                ck = f"ctb{bsel}"
                dma("gpsimd", ctb[bsel][:],
                    ctxT_d.ap().rearrange("(c p) t -> p c t", p=128)[
                        :, :, i * 128:(i + 1) * 128],
                    [ck] + [f"ctxd{h}_{i // 4}" for h in range(8)],
                    [ck], tag=f"p4c{i}")
                nmr, rstd, lk = ln_tile(xk, xb[bsel][:], i % 4, bsel)
                yk = f"yb{bsel}"
                A("vector", lambda e, b_=bsel, m=nmr, r=rstd: e.tensor_scalar(
                    yb[b_][:], xb[b_][:], r[:, 0:1], m[:, 0:1],
                    ALU.mult, ALU.add),
                  reads=[xk, lk], writes=[yk], tag="p4y")
                po = (i % 2) * 512
                for half in range(2):
                    pse = psA if half == 0 else psB
                    pk = (PA0 if half == 0 else PB0) if i % 2 == 0 \
                        else (PA1 if half == 0 else PB1)
                    for c in range(8):
                        A("tensor",
                          lambda e, b_=bsel, c=c, half=half, pse=pse, po=po:
                          e.matmul(pse[:, po:po + 512],
                                   ctb[b_][:, c * 128:(c + 1) * 128],
                                   wos[:, c * 1024 + half * 512:
                                       c * 1024 + (half + 1) * 512],
                                   start=(c == 0), stop=False),
                          reads=[ck, f"wos_h{c // 4}", pk], writes=[pk],
                          tag="womm")
                    A("tensor", lambda e, half=half, pse=pse, po=po:
                      e.matmul(
                        pse[:, po:po + 512], ones_row[:],
                        row_bf("bo", half * 512, 512), start=False, stop=True),
                      reads=["ones_row", "rows_b", pk], writes=[pk],
                      tag="wob")
                    A("vector", lambda e, half=half, pse=pse, b_=bsel, po=po:
                      e.tensor_add(zb4[:, half * 512:(half + 1) * 512],
                                   pse[:, po:po + 512],
                                   yb[b_][:, half * 512:(half + 1) * 512]),
                      reads=[pk, yk, "zb4"], writes=["zb4", pk], tag="x2")
                dma("sync", x2_d.ap()[i * 128:(i + 1) * 128, :], zb4[:],
                    ["zb4"], [f"x2d{i}"], tag="x2sp")
                if "x2" in taps:
                    dma("sync", taps["x2"].ap()[i * 128:(i + 1) * 128, :],
                        zb4[:], ["zb4"], ["tap_x2"])
                for half in range(2):
                    pse = psA if half == 0 else psB
                    pk = (PA0 if half == 0 else PB0) if i % 2 == 0 \
                        else (PA1 if half == 0 else PB1)
                    for c in range(8):
                        A("tensor",
                          lambda e, c=c, j=j, half=half, pse=pse, po=po:
                          e.matmul(pse[:, po:po + 512],
                                   xnb[0][:, c * 512 + j * 128:
                                       c * 512 + (j + 1) * 128],
                                   wgs[:, c * 1024 + half * 512:
                                       c * 1024 + (half + 1) * 512],
                                   start=(c == 0), stop=False),
                          reads=[f"xnb0_{c}", f"wgs_h{c // 4}", pk],
                          writes=[pk], tag="gmm")
                    A("tensor", lambda e, half=half, pse=pse, po=po:
                      e.matmul(
                        pse[:, po:po + 512], ones_row[:],
                        row_bf("bgate", half * 512, 512),
                        start=False, stop=True),
                      reads=["ones_row", "rows_b", pk], writes=[pk], tag="gb")
                    A("scalar", lambda e, half=half, pse=pse, po=po:
                      e.activation(
                        gsc[:, half * 512:(half + 1) * 512],
                        pse[:, po:po + 512],
                        AF.Tanh, scale=0.5),
                      reads=[pk, "gsc"], writes=["gsc", pk], tag="gth")
                A("vector", lambda e: e.tensor_scalar(
                    gtb[:], gsc[:], 0.5, 0.5, ALU.mult, ALU.add),
                  reads=["gsc", "gtb"], writes=["gtb"], tag="gfin")
                dma("sync", gate_d.ap()[i * 128:(i + 1) * 128, :],
                    gtb[:], ["gtb"], [f"gated{i}"], tag="gatesp")

        st4a.close()
        stR.close()
        p.epoch("p4b", lambda e: e.memset(epb[0:1, 0:1], 0.0))

        scan_state["target"] = 600.0
        st4b = ExitStack()
        sb4b = lambda n, sh, dt: st4b.enter_context(
            nc.sbuf_tensor(n, list(sh), dt))
        xn2T = sb4b("xn2Ts", (128, 8 * 1024), BF16)
        facc = sb4b("faccs", (128, 8 * 1024), F32)
        w1c = [sb4b("w1cs0", (128, 8 * 512), BF16)] * 2
        w2c = [sb4b(f"w2cs{i}", (128, 4 * 1024), BF16) for i in range(2)]
        h1b = [sb4b(f"h1bs{i}", (128, 4 * 1024), BF16) for i in range(2)]

        xbig4 = st4b.enter_context(
            nc.sbuf_tensor("xbig4s", [128, 4096], F32, side="right"))
        for th in range(2):
            for grp in range(2):
                for j in range(4):
                    i = th * 8 + grp * 4 + j
                    dma("sync", xbig4[:, j * 1024:(j + 1) * 1024],
                        x2_d.ap()[i * 128:(i + 1) * 128, :],
                        [f"xbig{j}", f"x2d{i}"], [f"xbig{j}"], tag="x2ld")
                nmrw, rstdw, lks, lco = ln_group(
                    [f"xbig{j}" for j in range(4)],
                    [xbig4[:, j * 1024:(j + 1) * 1024] for j in range(4)],
                    grp % 2)
                for j in range(4):
                    i8 = grp * 4 + j
                    i = th * 8 + i8
                    bsel = i8 % 2
                    A("vector", lambda e, b_=bsel, j=j, lco=lco:
                      e.tensor_scalar(
                        ybf[b_][:], xbig4[:, j * 1024:(j + 1) * 1024],
                        rstdw[:, lco + j:lco + j + 1],
                        nmrw[:, lco + j:lco + j + 1],
                        ALU.mult, ALU.add),
                      reads=[f"xbig{j}", f"ybf{bsel}"] + lks,
                      writes=[f"ybf{bsel}"], tag="x2h")
                    for r in range(2):
                        ps_, bkk, off = TRB3[(r + 2 * j) % 3]
                        for q in range(4):
                            c = r * 4 + q
                            A("tensor",
                              lambda e, b_=bsel, c=c, q=q, ps_=ps_, off=off:
                              e.matmul(
                                ps_[:, off + q * 128:off + (q + 1) * 128],
                                ybf[b_][:, c * 128:(c + 1) * 128], ident[:],
                                start=True, stop=True),
                              reads=[f"ybf{bsel}", "ident", bkk],
                              writes=[bkk], tag="x2tr")
                        for q in range(4):
                            c = r * 4 + q
                            A("vector",
                              lambda e, c=c, q=q, i8=i8, ps_=ps_, off=off,
                              bkk=bkk:
                              e.tensor_scalar(
                                  xn2T[:, c * 1024 + i8 * 128:
                                       c * 1024 + (i8 + 1) * 128],
                                  ps_[:, off + q * 128:off + (q + 1) * 128],
                                  lnp["ln2g"][:, c:c + 1],
                                  lnp["ln2b"][:, c:c + 1],
                                  ALU.mult, ALU.add),
                              reads=[bkk, "ln2g", "ln2b", "xn2T"],
                              writes=["xn2T"] + ([bkk] if q == 3 else []),
                              tag="xn2ev")
            for fc in range(8):
                fb = fc % 2
                w1k, w2k, h1k = "w1c0", f"w2c{fb}", f"h1b{fb}"
                for hf in range(2):
                    dma("gpsimd",
                        w1c[fb][:, hf * 2048:(hf + 1) * 2048].rearrange(
                            "p (c f) -> p c f", c=4),
                        dram["W1"].ap()[hf * 512:(hf + 1) * 512,
                                        fc * 512:(fc + 1) * 512].rearrange(
                            "(c p) f -> p c f", p=128),
                        [f"{w1k}_h{hf}"], [f"{w1k}_h{hf}"], tag="w1l")
                dma("gpsimd", w2c[fb][:].rearrange("p (c f) -> p c f", c=4),
                    dram["W2"].ap()[fc * 512:(fc + 1) * 512, :].rearrange(
                        "(c p) f -> p c f", p=128),
                    [w2k], [w2k], tag="w2l")
                for mc in range(4):
                    for tg in range(2):
                        pse, pk = (ps5a, P5A) if (mc * 2 + tg) % 2 == 0 \
                            else (ps5b, P5B)
                        for c in range(8):
                            A("tensor",
                              lambda e, c=c, mc=mc, tg=tg, fb=fb, pse=pse:
                              e.matmul(pse[:, 0:512],
                                       w1c[fb][:, c * 512 + mc * 128:
                                           c * 512 + (mc + 1) * 128],
                                       xn2T[:, c * 1024 + tg * 512:
                                            c * 1024 + (tg + 1) * 512],
                                       start=(c == 0), stop=(c == 7)),
                              reads=[f"{w1k}_h{c // 4}", "xn2T", pk],
                              writes=[pk], tag="w1mm")
                        A("vector",
                          lambda e, mc=mc, tg=tg, fb=fb, fc=fc, pse=pse:
                          e.tensor_scalar(h1b[fb][:, mc * 1024 + tg * 512:
                                                  mc * 1024 + (tg + 1) * 512],
                                          pse[:, 0:512],
                                          b1c[:, fc * 4 + mc:
                                              fc * 4 + mc + 1], 0.0,
                                          ALU.add, ALU.max),
                          reads=[pk, "b1c", h1k], writes=[h1k, pk],
                          tag="relu")
                for i8 in range(8):
                    po = (i8 % 2) * 512
                    for half in range(2):
                        pse = psA if half == 0 else psB
                        pk = (PA0 if half == 0 else PB0) if i8 % 2 == 0 \
                            else (PA1 if half == 0 else PB1)
                        for mc in range(4):
                            A("tensor",
                              lambda e, i8=i8, mc=mc, half=half, fb=fb,
                              pse=pse, fc=fc, po=po:
                              e.matmul(pse[:, po:po + 512],
                                       h1b[fb][:, mc * 1024 + i8 * 128:
                                           mc * 1024 + (i8 + 1) * 128],
                                       w2c[fb][:, mc * 1024 + half * 512:
                                           mc * 1024 + (half + 1) * 512],
                                       start=(mc == 0),
                                       stop=(mc == 3 and fc != 0)),
                              reads=[h1k, w2k, pk], writes=[pk], tag="w2mm")
                        if fc == 0:
                            A("tensor", lambda e, half=half, pse=pse, po=po:
                              e.matmul(pse[:, po:po + 512], ones_row[:],
                                       row_bf("b2", half * 512, 512),
                                       start=False, stop=True),
                              reads=["ones_row", "rows_b", pk], writes=[pk],
                              tag="b2b")
                            A("vector",
                              lambda e, i8=i8, half=half, pse=pse, po=po:
                              e.tensor_copy(
                                  facc[:, i8 * 1024 + half * 512:
                                       i8 * 1024 + (half + 1) * 512],
                                  pse[:, po:po + 512]),
                              reads=[pk, "facc"], writes=["facc", pk],
                              tag="facc0")
                        else:
                            A("vector",
                              lambda e, i8=i8, half=half, pse=pse, po=po:
                              e.tensor_add(
                                  facc[:, i8 * 1024 + half * 512:
                                       i8 * 1024 + (half + 1) * 512],
                                  facc[:, i8 * 1024 + half * 512:
                                       i8 * 1024 + (half + 1) * 512],
                                  pse[:, po:po + 512]),
                              reads=[pk, "facc"], writes=["facc", pk],
                              tag="facc")
            if "ffn" in taps and th == 0:
                tap_copy("ffn", facc[:], ["facc"])
            for i8 in range(8):
                i = th * 8 + i8
                bsel = i8 % 2
                xk = f"xb{bsel}"
                dma("sync", xb[bsel][:], x2_d.ap()[i * 128:(i + 1) * 128, :],
                    [xk, f"x2d{i}"], [xk], tag="x2ld2")
                A("vector", lambda e, b_=bsel, i8=i8: e.tensor_add(
                    yb[b_][:], xb[b_][:], facc[:, i8 * 1024:(i8 + 1) * 1024]),
                  reads=[xk, "facc", f"yb{bsel}"], writes=[f"yb{bsel}"],
                  tag="tout")
                A("vector", lambda e, b_=bsel: e.tensor_copy(ybf[b_][:],
                                                             yb[b_][:]),
                  reads=[f"yb{bsel}", f"ybf{bsel}"], writes=[f"ybf{bsel}"],
                  tag="tcast")
                transpose8(ybf[bsel][:], f"ybf{bsel}",
                           lambda r: tstage[:, 0:1024].rearrange(
                               "p (c t) -> p c t", c=8)[:, r * 4:(r + 1) * 4],
                           "tstage", banks=TRB3, evac="scalar")
                dma("sync",
                    moT_d.ap().rearrange("(c p) t -> p c t", p=128)[
                        :, 8:16, i * 128:(i + 1) * 128],
                    tstage[:, 0:1024].rearrange("p (c t) -> p c t", c=8),
                    ["tstage"], [f"moTd8{i}"], tag="tosp")

        st4b.close()

        st4bm = ExitStack()
        sb4bm = lambda n, sh, dt: st4bm.enter_context(
            nc.sbuf_tensor(n, list(sh), dt))
        gld = [sb4bm(f"glds{i}", (128, 1024), BF16) for i in range(2)]
        mob = sb4bm("mobs", (128, 1024), F32)
        for i in range(16):
            bsel = i % 2
            xk = f"xb{bsel}"
            gk = f"gld{bsel}"
            dma("sync", xb[bsel][:],
                dram["x_own"].ap()[i * 128:(i + 1) * 128, :], [xk], [xk],
                tag="bmx")
            dma("sync", gld[bsel][:], gate_d.ap()[i * 128:(i + 1) * 128, :],
                [gk, f"gated{i}"], [gk], tag="bmg")
            nmr, rstd, lk = ln_tile(xk, xb[bsel][:], i % 4, bsel)
            yk = f"yb{bsel}"
            A("vector", lambda e, b_=bsel, m=nmr, r=rstd: e.tensor_scalar(
                yb[b_][:], xb[b_][:], r[:, 0:1], m[:, 0:1],
                ALU.mult, ALU.add),
              reads=[xk, lk, yk], writes=[yk], tag="bmy")
            pump_scan(max(0, T + (i + 1) * 128 - scan_state["t"]))
            A("vector", lambda e, i=i: e.tensor_scalar_mul(
                hown[:, i * 128:(i + 1) * 128],
                hall[:, i * 128:(i + 1) * 128], sel0[0:16, 0:1]),
              reads=[f"h{i * 128 + 127}", "sel0", "hown"],
              writes=["hown"], tag="hsel0")
            A("vector", lambda e, i=i: e.scalar_tensor_tensor(
                hown[:, i * 128:(i + 1) * 128],
                hall[:, T + i * 128:T + (i + 1) * 128],
                sel1[0:16, 0:1], hown[:, i * 128:(i + 1) * 128],
                ALU.mult, ALU.add),
              reads=[f"h{T + i * 128 + 127}", "sel1", "hown"],
              writes=["hown"], tag="hsel1")
            for half in range(2):
                pse = psA if half == 0 else psB
                pk = PA0 if half == 0 else PB0
                A("tensor", lambda e, i=i, half=half, pse=pse: e.matmul(
                    pse[:, 0:512], hown[:, i * 128:(i + 1) * 128],
                    C_bf[:, half * 512:(half + 1) * 512],
                    start=True, stop=True),
                  reads=["hown", "C_bf", pk], writes=[pk], tag="hC")
                A("vector", lambda e, half=half, pse=pse, b_=bsel:
                  e.tensor_mul(
                      mob[:, half * 512:(half + 1) * 512], pse[:, 0:512],
                      gld[b_][:, half * 512:(half + 1) * 512]),
                  reads=[pk, gk, "mob"], writes=["mob", pk], tag="momul")
            A("vector", lambda e, b_=bsel: e.tensor_add(
                mob[:], mob[:], yb[b_][:]),
              reads=["mob", yk], writes=["mob"], tag="moadd")
            A("scalar", lambda e, b_=bsel: e.activation(
                ybf[b_][:], mob[:], AF.Identity),
              reads=["mob", f"ybf{bsel}"], writes=[f"ybf{bsel}"],
              tag="mocast")
            transpose8(ybf[bsel][:], f"ybf{bsel}",
                       lambda r: tstage[:, 0:1024].rearrange(
                           "p (c t) -> p c t", c=8)[:, r * 4:(r + 1) * 4],
                       "tstage", banks=TRB3, evac="scalar")
            dma("sync",
                moT_d.ap().rearrange("(c p) t -> p c t", p=128)[
                    :, 0:8, i * 128:(i + 1) * 128],
                tstage[:, 0:1024].rearrange("p (c t) -> p c t", c=8),
                ["tstage"], [f"moTd{i}"], tag="mosp")
        st4bm.close()
        stRs.close()
        p.epoch("p4c", lambda e: e.memset(epb[0:1, 0:1], 0.0))

        st4c = ExitStack()
        sb4c = lambda n, sh, dt: st4c.enter_context(
            nc.sbuf_tensor(n, list(sh), dt))
        wfs = sb4c("wfss", (128, 16 * 1024), BF16)
        lnfg = sb4c("lnfgs", (128, 1024), F32)
        lnfb = sb4c("lnfbs", (128, 1024), F32)
        mtb = [sb4c(f"mtbs{i}", (128, 2048), BF16) for i in range(2)]
        zb4c = sb4c("zb4cs", (128, 1024), F32)
        dma("gpsimd", wfs[:].rearrange("p (c f) -> p c f", c=16),
            dram["Wf"].ap().rearrange("(c p) f -> p c f", p=128),
            [], ["wfs"], tag="wf")
        dma("sync", lnfg[:], dram["lnfg_rep"].ap(), [], ["lnfg"])
        dma("sync", lnfb[:], dram["lnfb_rep"].ap(), [], ["lnfb"])

        zbig = st4c.enter_context(
            nc.sbuf_tensor("zbigs", [128, 4096], F32))
        for grp in range(4):
            for j in range(4):
                i = grp * 4 + j
                bsel = i % 2
                mk = f"mtb{bsel}"
                dma("gpsimd", mtb[bsel][:],
                    moT_d.ap().rearrange("(c p) t -> p c t", p=128)[
                        :, :, i * 128:(i + 1) * 128],
                    [mk, f"moTd{i}", f"moTd8{i}"], [mk], tag="mold")
                for half in range(2):
                    pse = psA if half == 0 else psB
                    pk = (PA0 if half == 0 else PB0) if i % 2 == 0 \
                        else (PA1 if half == 0 else PB1)
                    po = (i % 2) * 512
                    for c in range(16):
                        A("tensor",
                          lambda e, c=c, half=half, pse=pse, b_=bsel, po=po:
                          e.matmul(pse[:, po:po + 512],
                                   mtb[b_][:, c * 128:(c + 1) * 128],
                                   wfs[:, c * 1024 + half * 512:
                                       c * 1024 + (half + 1) * 512],
                                   start=(c == 0), stop=False),
                          reads=[mk, "wfs", pk], writes=[pk], tag="wfmm")
                    A("tensor", lambda e, half=half, pse=pse, po=po:
                      e.matmul(
                        pse[:, po:po + 512], ones_row[:],
                        row_bf("bf", half * 512, 512),
                        start=False, stop=True),
                      reads=["ones_row", "rows_b", pk], writes=[pk],
                      tag="bfb")
                    A("scalar",
                      lambda e, half=half, pse=pse, po=po, j=j:
                      e.activation(
                          zbig[:, j * 1024 + half * 512:
                               j * 1024 + (half + 1) * 512],
                          pse[:, po:po + 512], AF.Identity),
                      reads=[pk, f"zbig{j}"], writes=[f"zbig{j}", pk],
                      tag="fev")
            nmrw, rstdw, lks, lco = ln_group(
                [f"zbig{j}" for j in range(4)],
                [zbig[:, j * 1024:(j + 1) * 1024] for j in range(4)],
                grp % 2)
            for j in range(4):
                i = grp * 4 + j
                bsel = i % 2
                xk = f"xb{bsel}"
                A("vector", lambda e, b_=bsel, j=j, lco=lco: e.tensor_scalar(
                    yb[b_][:], zbig[:, j * 1024:(j + 1) * 1024],
                    rstdw[:, lco + j:lco + j + 1],
                    nmrw[:, lco + j:lco + j + 1], ALU.mult, ALU.add),
                  reads=[f"zbig{j}", f"yb{bsel}"] + lks,
                  writes=[f"yb{bsel}"], tag="fnorm")
                dma("sync", xb[bsel][:],
                    dram["x_own"].ap()[i * 128:(i + 1) * 128, :], [xk], [xk],
                    tag="p4cx")
                A("vector", lambda e, b_=bsel: e.tensor_mul(yb[b_][:],
                                                            yb[b_][:],
                                                            lnfg[:]),
                  reads=[f"yb{bsel}", "lnfg"], writes=[f"yb{bsel}"], tag="fg")
                A("vector", lambda e, b_=bsel: e.tensor_add(yb[b_][:],
                                                            yb[b_][:],
                                                            lnfb[:]),
                  reads=[f"yb{bsel}", "lnfb"], writes=[f"yb{bsel}"],
                  tag="fbias")
                A("vector", lambda e, b_=bsel, j=j: e.tensor_add(
                    zbig[:, j * 1024:(j + 1) * 1024], yb[b_][:], xb[b_][:]),
                  reads=[f"yb{bsel}", xk, f"zbig{j}"], writes=[f"zbig{j}"],
                  tag="o1")
                dma("sync", out1_d.ap()[i * 128:(i + 1) * 128, :],
                    zbig[:, j * 1024:(j + 1) * 1024],
                    [f"zbig{j}"], [f"o1d{i}"], tag="o1sp")
                if "out1" in taps:
                    dma("sync", taps["out1"].ap()[i * 128:(i + 1) * 128, :],
                        zbig[:, j * 1024:(j + 1) * 1024], [f"zbig{j}"],
                        ["tap_out1"])
            nmr2, rstd2, lks2, lco2 = ln_group(
                [f"zbig{j}" for j in range(4)],
                [zbig[:, j * 1024:(j + 1) * 1024] for j in range(4)],
                (grp + 1) % 2)
            for j in range(4):
                i = grp * 4 + j
                bsel = i % 2
                A("vector", lambda e, b_=bsel, j=j, lco2=lco2:
                  e.tensor_scalar(
                    ybf[b_][:], zbig[:, j * 1024:(j + 1) * 1024],
                    rstd2[:, lco2 + j:lco2 + j + 1],
                    nmr2[:, lco2 + j:lco2 + j + 1], ALU.mult, ALU.add),
                  reads=[f"zbig{j}", f"ybf{bsel}"] + lks2,
                  writes=[f"ybf{bsel}"], tag="y2h")
                transpose8(ybf[bsel][:], f"ybf{bsel}",
                           lambda r: tstage[:, 0:1024].rearrange(
                               "p (c t) -> p c t", c=8)[:, r * 4:(r + 1) * 4],
                           "tstage", banks=TRB3, evac="scalar")
                dma("sync",
                    y2T_d.ap().rearrange("(c p) t -> p c t", p=128)[
                        :, :, i * 128:(i + 1) * 128],
                    tstage[:, 0:1024].rearrange("p (c t) -> p c t", c=8),
                    ["tstage"], [f"y2Td{i}"], tag="y2sp")
        tap_copy("y2T", y2T_d.ap(), [f"y2Td{i}" for i in range(16)])

        st4c.close()
        p.epoch("p5", lambda e: e.memset(epb[0:1, 0:1], 0.0))

        st5 = ExitStack()
        sb5 = lambda n, sh, dt: st5.enter_context(
            nc.sbuf_tensor(n, list(sh), dt))
        y2h = sb5("y2hs", (128, 8 * 1024), BF16)
        acc = sb5("accs", (128, 8 * 1024), F32)
        glub = sb5("glubs", (128, 16 * 1024), BF16)
        w1e = [sb5(f"w1es{i}", (128, 8 * 512), BF16) for i in range(2)]
        w2e = [sb5(f"w2es{i}", (128, 8 * 512), BF16) for i in range(2)]
        w3e = sb5("w3es", (128, 16 * 1024), BF16)
        s1b = sb5("s1bs", (128, 1024), BF16)
        pr8 = sb5("pr8s", (128, 8), F32)
        wtse = sb5("wtses", (128, 16), BF16)
        wtsT = sb5("wtsTs", (16, 128), BF16)
        b3all = sb5("b3alls", (16, 1024), BF16)

        dma("gpsimd", b3all[0:8, :], dram["rb3"].ap(),
            ["b3all"], ["b3all"], tag="b3la")
        dma("gpsimd", b3all[8:10, :], dram["sb3"].ap(),
            ["b3all"], ["b3all"], tag="b3lb")
        A("vector", lambda e: e.memset(wtse[:, 8:10], 0.5),
          reads=["wtse"], writes=["wtse8"], tag="wtsm")

        for th in range(2):
            for i8 in range(8):
                dma("sync",
                    y2h[:].rearrange("p (c t) -> p c t", c=8)[
                        :, :, i8 * 128:(i8 + 1) * 128],
                    y2T_d.ap().rearrange("(c p) t -> p c t", p=128)[
                        :, :, th * 1024 + i8 * 128:
                        th * 1024 + (i8 + 1) * 128],
                    [f"y2Td{th * 8 + i8}", f"y2h{i8}"],
                    [f"y2h{i8}"], tag="y2ld")
            def emit_gate(i8, th=th):
                gi = th * 8 + i8
                for c in range(8):
                    A("tensor", lambda e, c=c, i8=i8: e.matmul(
                        ps5a[:, 0:8],
                        y2h[:, c * 1024 + i8 * 128:c * 1024 + (i8 + 1) * 128],
                        gatew[:, c * 8:(c + 1) * 8],
                        start=(c == 0), stop=False),
                      reads=[f"y2h{i8}", "gatew", P5A], writes=[P5A],
                      tag="gsc")
                A("tensor", lambda e: e.matmul(
                    ps5a[:, 0:8], ones_row[:], row_bf("gateb", 0, 8),
                    start=False, stop=True),
                  reads=["ones_row", "rows_b", P5A], writes=[P5A], tag="gscb")
                A("vector", lambda e: e.reduce_max(st1[0][:, 0:1],
                                                   ps5a[:, 0:8],
                                                   axis=AX.X, negate=True),
                  reads=[P5A, "LNS0"], writes=["LNS0"], tag="gmax")
                A("scalar", lambda e: e.activation(pr8[:], ps5a[:, 0:8],
                                                   AF.Exp,
                                                   bias=st1[0][:, 0:1],
                                                   accum_out=st2[0][:, 0:1]),
                  reads=[P5A, "LNS0", "pr8"], writes=["pr8", "LNS0", P5A],
                  tag="gexp")
                A("vector", lambda e: e.reciprocal(st2[0][:, 0:1],
                                                   st2[0][:, 0:1]),
                  reads=["LNS0"], writes=["LNS0"], tag="grec")
                A("vector", lambda e: e.tensor_scalar_mul(
                    pr8[:], pr8[:], st2[0][:, 0:1]),
                  reads=["pr8", "LNS0"], writes=["pr8"], tag="gp")
                A("vector", lambda e: e.reduce_max(st1[0][:, 0:1], pr8[:],
                                                   axis=AX.X),
                  reads=["pr8", "LNS0"], writes=["LNS0"], tag="gm1")
                A("vector", lambda e: e.tensor_scalar(
                    dump[0][:, 0:8], pr8[:], st1[0][:, 0:1], None, ALU.is_ge),
                  reads=["pr8", "LNS0", "dmp0"], writes=["dmp0"], tag="gis1")
                A("vector", lambda e: e.scalar_tensor_tensor(
                    yb[0][:, 0:8], dump[0][:, 0:8], -2.0, pr8[:], ALU.mult,
                    ALU.add),
                  reads=["dmp0", "pr8", "yb0"], writes=["yb0"], tag="gp2")
                A("vector", lambda e: e.reduce_max(st1[0][:, 0:1],
                                                   yb[0][:, 0:8], axis=AX.X),
                  reads=["yb0", "LNS0"], writes=["LNS0"], tag="gm2")
                A("vector", lambda e: e.tensor_scalar(
                    dump[0][:, 0:8], pr8[:], st1[0][:, 0:1], None, ALU.is_ge),
                  reads=["pr8", "LNS0", "dmp0"], writes=["dmp0"], tag="gis2")
                A("vector", lambda e, gi=gi: e.tensor_mul(
                    wts[:, gi * 8:(gi + 1) * 8], pr8[:], dump[0][:, 0:8]),
                  reads=["pr8", "dmp0", "wts"], writes=["wts"], tag="gwts")

            dma("sync", acc[:].rearrange("p (i d) -> p i d", i=8),
                out1_d.ap().rearrange("(i p) d -> p i d", p=128)[
                    :, th * 8:(th + 1) * 8, :],
                [f"o1d{i}" for i in range(th * 8, th * 8 + 8)] + ["acc"],
                ["acc"], tag="acci")
            pending_gates = list(range(8))
            for ex in range(10):
                routed = ex < 8
                wsrc1 = dram["rw1"] if routed else dram["sw1"]
                wsrc2 = dram["rw2"] if routed else dram["sw2"]
                wsrc3 = dram["rw3"] if routed else dram["sw3"]
                bsrc3 = dram["rb3"] if routed else dram["sb3"]
                eidx = ex if routed else ex - 8
                bias1 = rb1c if routed else sb1c
                bias2 = rb2c if routed else sb2c
                for hc in range(2):
                    wb = hc % 2
                    dma("gpsimd", w1e[wb][:].rearrange("p (c f) -> p c f", c=8),
                        wsrc1.ap()[eidx][:, hc * 512:(hc + 1) * 512]
                        .rearrange("(c p) f -> p c f", p=128),
                        [f"w1e{wb}"], [f"w1e{wb}"], tag="w1e")
                    dma("gpsimd", w2e[wb][:].rearrange("p (c f) -> p c f", c=8),
                        wsrc2.ap()[eidx][:, hc * 512:(hc + 1) * 512]
                        .rearrange("(c p) f -> p c f", p=128),
                        [f"w2e{wb}"], [f"w2e{wb}"], tag="w2e")
                dma("gpsimd", w3e[:].rearrange("p (c f) -> p c f", c=16),
                    wsrc3.ap()[eidx].rearrange("(c p) f -> p c f", p=128),
                    ["w3e"], ["w3e"], tag="w3l")
                for hc in range(4):
                    wb = hc % 2
                    w1k, w2k = f"w1e{wb}", f"w2e{wb}"
                    if hc >= 2:
                        dma("gpsimd", w1e[wb][:].rearrange("p (c f) -> p c f", c=8),
                            wsrc1.ap()[eidx][:, hc * 512:(hc + 1) * 512]
                            .rearrange("(c p) f -> p c f", p=128),
                            [w1k], [w1k], tag="w1e")
                        dma("gpsimd", w2e[wb][:].rearrange("p (c f) -> p c f", c=8),
                            wsrc2.ap()[eidx][:, hc * 512:(hc + 1) * 512]
                            .rearrange("(c p) f -> p c f", p=128),
                            [w2k], [w2k], tag="w2e")
                    for mc in range(4):
                        mcg = hc * 4 + mc
                        for tg in range(2):
                            par = (mc * 2 + tg) % 2
                            po = par * 512
                            pka = PA0 if par == 0 else PA1
                            pkb = PB0 if par == 0 else PB1
                            for c in range(8):
                                A("tensor",
                                  lambda e, c=c, mc=mc, tg=tg, wb=wb, po=po:
                                  e.matmul(psA[:, po:po + 512],
                                           w1e[wb][:, c * 512 + mc * 128:
                                               c * 512 + (mc + 1) * 128],
                                           y2h[:, c * 1024 + tg * 512:
                                               c * 1024 + (tg + 1) * 512],
                                           start=(c == 0), stop=(c == 7)),
                                  reads=[f"y2h{tg * 4}", f"y2h{tg * 4 + 1}",
                                         f"y2h{tg * 4 + 2}",
                                         f"y2h{tg * 4 + 3}", w1k, pka],
                                  writes=[pka], tag="p1mm")
                            for c in range(8):
                                A("tensor",
                                  lambda e, c=c, mc=mc, tg=tg, wb=wb, po=po:
                                  e.matmul(psB[:, po:po + 512],
                                           w2e[wb][:, c * 512 + mc * 128:
                                               c * 512 + (mc + 1) * 128],
                                           y2h[:, c * 1024 + tg * 512:
                                               c * 1024 + (tg + 1) * 512],
                                           start=(c == 0), stop=(c == 7)),
                                  reads=[f"y2h{tg * 4}", f"y2h{tg * 4 + 1}",
                                         f"y2h{tg * 4 + 2}",
                                         f"y2h{tg * 4 + 3}", w2k, pkb],
                                  writes=[pkb], tag="p2mm")
                            A("scalar",
                              lambda e, mcg=mcg, tg=tg, eidx=eidx, b_=bias1,
                              po=po:
                              e.activation(s1b[:, tg * 512:(tg + 1) * 512],
                                           psA[:, po:po + 512], AF.Silu,
                                           bias=b_[:, eidx * 16 + mcg:
                                                   eidx * 16 + mcg + 1]),
                              reads=[pka, "s1b"], writes=["s1b", pka],
                              tag="silu")
                            A("vector",
                              lambda e, mcg=mcg, tg=tg, eidx=eidx, b_=bias2,
                              po=po:
                              e.scalar_tensor_tensor(
                                  glub[:, mcg * 1024 + tg * 512:
                                       mcg * 1024 + (tg + 1) * 512],
                                  psB[:, po:po + 512],
                                  b_[:, eidx * 16 + mcg:eidx * 16 + mcg + 1],
                                  s1b[:, tg * 512:(tg + 1) * 512],
                                  ALU.add, ALU.mult),
                              reads=[pkb, "s1b", "glub"],
                              writes=["glub", pkb], tag="glu")
                            if ex == 0 and (mc * 2 + tg) % 2 == 1 \
                                    and pending_gates:
                                emit_gate(pending_gates.pop(0))
                for i8 in range(8):
                    gi8 = th * 8 + i8
                    if ex == 0:
                        A("vector", lambda e, gi8=gi8: e.tensor_copy(
                            wtse[:, 0:8], wts[:, gi8 * 8:(gi8 + 1) * 8]),
                          reads=["wts", "wtse"], writes=["wtse"],
                          tag="wtsec")
                        A("tensor", lambda e: e.matmul(
                            psT[0:10, 0:128], wtse[:, 0:10], ident[:],
                            start=True, stop=True),
                          reads=["wtse", "wtse8", "ident", PT],
                          writes=[PT], tag="wtr")
                        A("vector", lambda e: e.tensor_copy(
                            wtsT[0:10, :], psT[0:10, 0:128]),
                          reads=[PT, "wtsT"], writes=["wtsT", PT],
                          tag="wtev")
                        for hf2 in range(2):
                            A("tensor", lambda e, hf2=hf2: e.matmul(
                                psT[:, 0:512], wtsT[0:10, :],
                                b3all[0:10, hf2 * 512:(hf2 + 1) * 512],
                                start=True, stop=True),
                              reads=["wtsT", "b3all", PT], writes=[PT],
                              tag="b3pre")
                            A("vector", lambda e, i8=i8, hf2=hf2:
                              e.tensor_add(
                                  acc[:, i8 * 1024 + hf2 * 512:
                                      i8 * 1024 + (hf2 + 1) * 512],
                                  acc[:, i8 * 1024 + hf2 * 512:
                                      i8 * 1024 + (hf2 + 1) * 512],
                                  psT[:, 0:512]),
                              reads=[PT, "acc"], writes=["acc", PT],
                              tag="b3acc")
                    for hf in range(2):
                        pse, pk = (ps5a, P5A) if hf == 0 else (ps5b, P5B)
                        for c in range(16):
                            A("tensor", lambda e, c=c, i8=i8, hf=hf, pse=pse:
                              e.matmul(pse[:, 0:512],
                                       glub[:, c * 1024 + i8 * 128:
                                            c * 1024 + (i8 + 1) * 128],
                                       w3e[:, c * 1024 + hf * 512:
                                           c * 1024 + (hf + 1) * 512],
                                       start=(c == 0), stop=(c == 15)),
                              reads=["glub", "w3e", pk], writes=[pk],
                              tag="w3mm")
                        gi = th * 8 + i8
                        if routed:
                            A("vector",
                              lambda e, i8=i8, hf=hf, gi=gi, ex=ex, pse=pse:
                              e.scalar_tensor_tensor(
                                  acc[:, i8 * 1024 + hf * 512:
                                      i8 * 1024 + (hf + 1) * 512],
                                  pse[:, 0:512],
                                  wts[:, gi * 8 + ex:gi * 8 + ex + 1],
                                  acc[:, i8 * 1024 + hf * 512:
                                      i8 * 1024 + (hf + 1) * 512],
                                  ALU.mult, ALU.add),
                              reads=[pk, "wts", "acc"], writes=["acc", pk],
                              tag="accr")
                        else:
                            A("vector", lambda e, i8=i8, hf=hf, pse=pse:
                              e.scalar_tensor_tensor(
                                  acc[:, i8 * 1024 + hf * 512:
                                      i8 * 1024 + (hf + 1) * 512],
                                  pse[:, 0:512], 0.5,
                                  acc[:, i8 * 1024 + hf * 512:
                                      i8 * 1024 + (hf + 1) * 512],
                                  ALU.mult, ALU.add),
                              reads=[pk, "acc"], writes=["acc", pk],
                              tag="accs")
            for i8 in range(8):
                gi = th * 8 + i8
                dma("scalar", out_h.ap()[gi * 128:(gi + 1) * 128, :],
                    acc[:, i8 * 1024:(i8 + 1) * 1024],
                    ["acc"], [f"outd{gi}"], tag="outw")
        tap_copy("wts", wts[:], ["wts"])
        st5.close()

        p.barrier_all("sync")
        p.emit(nc, st0)

    return nc, taps


_CACHE = {}


def _prep_core_inputs(inputs, b, h):
    f32 = np.float32

    def asf(a):
        return np.ascontiguousarray(a, dtype=f32)

    def chunks(v, nc_):
        return np.ascontiguousarray(np.asarray(v, f32).reshape(nc_, 128).T)

    x = np.asarray(inputs["L0_attn_x"], f32)
    im = {}
    im["x_full"] = asf(x[b])
    im["x_own"] = asf(x[b, h * T:(h + 1) * T])
    im["sel0"] = np.full((128, 1), 1.0 if h == 0 else 0.0, f32)
    im["sel1"] = np.full((128, 1), 0.0 if h == 0 else 1.0, f32)
    im["ident"] = np.eye(128, dtype=BF16NP)
    im["ones_row"] = np.ones((1, 128), dtype=BF16NP)
    im["lng_c"] = chunks(inputs["L0_attn_ln_g"], 8)
    im["lnb_c"] = chunks(inputs["L0_attn_ln_b"], 8)
    im["ln1g_c"] = chunks(inputs["L0_attn_ln1_g"], 8)
    im["ln1b_c"] = chunks(inputs["L0_attn_ln1_b"], 8)
    im["ln2g_c"] = chunks(inputs["L0_attn_ln2_g"], 8)
    im["ln2b_c"] = chunks(inputs["L0_attn_ln2_b"], 8)
    im["bq_c"] = chunks(inputs["L0_attn_bq"], 8)
    im["bk_c"] = chunks(inputs["L0_attn_bk"], 8)
    im["b1_c"] = chunks(inputs["L0_attn_b1"], 32)
    im["rb1_c"] = np.ascontiguousarray(
        np.asarray(inputs["L0_ffn_rb1"], f32).reshape(8, 16, 128)
        .transpose(2, 0, 1).reshape(128, 128))
    im["rb2_c"] = np.ascontiguousarray(
        np.asarray(inputs["L0_ffn_rb2"], f32).reshape(8, 16, 128)
        .transpose(2, 0, 1).reshape(128, 128))
    im["sb1_c"] = np.ascontiguousarray(
        np.asarray(inputs["L0_ffn_sb1"], f32).reshape(2, 16, 128)
        .transpose(2, 0, 1).reshape(128, 32))
    im["sb2_c"] = np.ascontiguousarray(
        np.asarray(inputs["L0_ffn_sb2"], f32).reshape(2, 16, 128)
        .transpose(2, 0, 1).reshape(128, 32))
    im["lnfg_rep"] = np.ascontiguousarray(np.broadcast_to(
        np.asarray(inputs["L0_attn_lnf_g"], f32), (128, 1024)))
    im["lnfb_rep"] = np.ascontiguousarray(np.broadcast_to(
        np.asarray(inputs["L0_attn_lnf_b"], f32), (128, 1024)))
    im["bgate_row"] = asf(inputs["L0_attn_bgate"]).reshape(1, -1)
    im["bv_row"] = asf(inputs["L0_attn_bv"]).reshape(1, -1)
    im["bo_row"] = asf(inputs["L0_attn_bo"]).reshape(1, -1)
    im["b2_row"] = asf(inputs["L0_attn_b2"]).reshape(1, -1)
    im["bf_row"] = asf(inputs["L0_attn_bf"]).reshape(1, -1)
    im["rb3"] = asf(inputs["L0_ffn_rb3"])
    im["sb3"] = asf(inputs["L0_ffn_sb3"])
    im["gateb_row"] = asf(inputs["L0_ffn_gate_b"]).reshape(1, -1)
    im["A_f"] = asf(inputs["L0_attn_A"])
    im["bdelta_c"] = asf(inputs["L0_attn_bdelta"]).reshape(16, 1)
    im["bin_c"] = asf(inputs["L0_attn_b_in"]).reshape(16, 1)
    im["wdelta_r"] = np.ascontiguousarray(
        asf(inputs["L0_attn_Wdelta"]).reshape(8, 128, 16)
        .transpose(1, 0, 2).reshape(128, 128))
    im["win_r"] = np.ascontiguousarray(
        asf(inputs["L0_attn_Win"]).reshape(8, 128, 16)
        .transpose(1, 0, 2).reshape(128, 128))
    im["gatew_r"] = np.ascontiguousarray(
        asf(inputs["L0_ffn_gate_w"]).reshape(8, 128, 8)
        .transpose(1, 0, 2).reshape(128, 64))
    im["C_w"] = asf(inputs["L0_attn_C"])
    for nm, key in (("Wq", "L0_attn_Wq"), ("Wk", "L0_attn_Wk"),
                    ("Wv", "L0_attn_Wv"), ("Wo", "L0_attn_Wo"),
                    ("Wgate", "L0_attn_Wgate"), ("W1", "L0_attn_W1"),
                    ("W2", "L0_attn_W2"), ("Wf", "L0_attn_Wf"),
                    ("rw1", "L0_ffn_rw1"), ("rw2", "L0_ffn_rw2"),
                    ("rw3", "L0_ffn_rw3"), ("sw1", "L0_ffn_sw1"),
                    ("sw2", "L0_ffn_sw2"), ("sw3", "L0_ffn_sw3")):
        im[nm] = asf(inputs[key])
    return im


def kernel(**inputs):
    if "nc" not in _CACHE:
        _CACHE["nc"] = build_nc()
    nc, _taps = _CACHE["nc"]
    in_maps = [_prep_core_inputs(inputs, c // 2, c % 2) for c in range(8)]
    res = run_bass_kernel_spmd(nc, in_maps, core_ids=list(range(8)))
    out = np.empty((B, S, D), np.float32)
    for c in range(8):
        out[c // 2, (c % 2) * T:(c % 2 + 1) * T, :] = np.asarray(
            res.results[c]["out"], np.float32)
    return out
